# version 10 (final)
# baseline (speedup 1.0000x reference)
import math
import contextlib
import numpy as np
import concourse.bass as bass
import concourse.mybir as mybir
from concourse.bass_utils import run_bass_kernel_spmd

F32 = mybir.dt.float32
BF16 = mybir.dt.bfloat16
I32 = mybir.dt.int32
AF = mybir.ActivationFunctionType
ALU = mybir.AluOpType
AX = mybir.AxisListType

B, S, D = 4, 4096, 2048
H, NOPE, ROPE, VD, RANK = 16, 128, 64, 128, 512
QHD = NOPE + ROPE
DH = 8192
SCALE = float(np.float32(1.0) / np.sqrt(np.float32(QHD)))
NEG = -1.0e20


def _ds(x):
    hi = np.float32(x)
    lo = np.float32(x - np.float64(hi))
    return float(hi), float(lo)


INV2PI_HI, INV2PI_LO = _ds(1.0 / (2 * math.pi))
TWOPI_HI, TWOPI_LO = _ds(2 * math.pi)
PIHALF = float(np.float32(math.pi / 2))

ENGINES = ("sync", "tensor", "vector", "scalar", "gpsimd")
V, Sc, T, G, Sy = "vector", "scalar", "tensor", "gpsimd", "sync"


class SemCtr:
    def __init__(self, name):
        self.name = name
        self.total = 0


class KB:

    def __init__(self, nc):
        self.nc = nc
        self.ops = {e: [] for e in ENGINES}
        self.sems = {}
        self.sem_handles = {}
        self.ninstr = 0

    def sem(self, name):
        if name not in self.sems:
            self.sems[name] = SemCtr(name)
        return self.sems[name]

    def wait(self, eng, sem_name, value=None):
        ctr = self.sem(sem_name)
        v = ctr.total if value is None else value
        assert v <= ctr.total, (sem_name, v, ctr.total)
        if v <= 0:
            return
        self.ops[eng].append(lambda e, _s=sem_name, _v=v: e.wait_ge(self.sem_handles[_s], _v))

    def op(self, eng, method, *args, inc=None, **kwargs):
        self.ninstr += 1
        if inc is None:
            self.ops[eng].append(lambda e: getattr(e, method)(*args, **kwargs))
            return None
        sem_name, n = inc
        ctr = self.sem(sem_name)
        ctr.total += n
        def g(e, _m=method, _a=args, _k=kwargs, _s=sem_name, _n=n):
            getattr(e, _m)(*_a, **_k).then_inc(self.sem_handles[_s], _n)
        self.ops[eng].append(g)
        return ctr.total

    def dma(self, eng, out_ap, in_ap, sem_name):
        return self.op(eng, "dma_start", out=out_ap, in_=in_ap, inc=(sem_name, 16))

    def barrier(self):
        for eng in ENGINES:
            for sem_name, ctr in self.sems.items():
                if ctr.total > 0:
                    self.wait(eng, sem_name)

    def finalize(self):
        nc = self.nc
        with contextlib.ExitStack() as stack:
            for name in self.sems:
                self.sem_handles[name] = stack.enter_context(nc.semaphore(name))
            with nc.Block() as block:
                for eng_name in ENGINES:
                    fns = self.ops[eng_name]
                    if fns:
                        getattr(block, eng_name)(lambda e, _f=fns: [f(e) for f in _f])


class Arena:
    def __init__(self, tensor, words, name):
        self.t = tensor
        self.name = name
        self.regions = {}
        self.free_list = [(0, words)]
        self.peak = 0

    def alloc(self, name, size):
        assert name not in self.regions, name
        for i, (off, sz) in enumerate(self.free_list):
            if sz >= size:
                self.regions[name] = (off, size)
                if sz == size:
                    self.free_list.pop(i)
                else:
                    self.free_list[i] = (off + size, sz - size)
                self.peak = max(self.peak, max(o + s for (o, s) in
                                               [(v[0], v[1]) for v in self.regions.values()]))
                return off
        raise MemoryError(f"{self.name}: alloc {name}[{size}] failed; free={self.free_list}")

    def free(self, *names):
        for name in names:
            off, size = self.regions.pop(name)
            self.free_list.append((off, size))
        self.free_list.sort()
        merged = []
        for off, sz in self.free_list:
            if merged and merged[-1][0] + merged[-1][1] == off:
                merged[-1] = (merged[-1][0], merged[-1][1] + sz)
            else:
                merged.append((off, sz))
        self.free_list = merged

    def ap(self, name, lo=0, hi=None, p0=0, p1=128):
        off, size = self.regions[name]
        hi = size if hi is None else hi
        assert 0 <= lo <= hi <= size, (name, lo, hi, size)
        return self.t[p0:p1, off + lo: off + hi]


def build_program(phases=("P0a", "P0b", "P1", "P2", "P3", "P4"), debug_dumps=(),
                  p3_heads=H, p4_tiles=8):
    nc = bass.Bass("TRN2", target_bir_lowering=False)

    xfull_d = nc.dram_tensor("xfull", [S, D], F32, kind="ExternalInput").ap()
    xown_d = nc.dram_tensor("xown", [S // 2, D], F32, kind="ExternalInput").ap()
    qpos_d = nc.dram_tensor("qpos", [128, 16], F32, kind="ExternalInput").ap()
    qposf_d = nc.dram_tensor("qposf", [64, S // 2], F32, kind="ExternalInput").ap()
    wqT_d = nc.dram_tensor("wqT", [D, H * QHD], F32, kind="ExternalInput").ap()
    waT_d = nc.dram_tensor("waT", [D, RANK + ROPE], F32, kind="ExternalInput").ap()
    wkvbT_d = nc.dram_tensor("wkvbT", [RANK, H * (NOPE + VD)], F32, kind="ExternalInput").ap()
    woT_d = nc.dram_tensor("woT", [H * VD, D], F32, kind="ExternalInput").ap()
    w1T_d = nc.dram_tensor("w1T", [D, DH], F32, kind="ExternalInput").ap()
    w2T_d = nc.dram_tensor("w2T", [D, DH], F32, kind="ExternalInput").ap()
    w3T_d = nc.dram_tensor("w3T", [DH, D], F32, kind="ExternalInput").ap()
    kvnw_d = nc.dram_tensor("kvnw", [1, RANK], F32, kind="ExternalInput").ap()
    g1_d = nc.dram_tensor("g1", [1, D], F32, kind="ExternalInput").ap()
    g2_d = nc.dram_tensor("g2", [1, D], F32, kind="ExternalInput").ap()
    g3_d = nc.dram_tensor("g3", [1, DH], F32, kind="ExternalInput").ap()
    out_d = nc.dram_tensor("out", [S // 2, D], F32, kind="ExternalOutput").ap()

    qsp_d = nc.dram_tensor("qsp", [H * QHD, S // 2], F32, kind="Internal").ap()
    aosp_d = nc.dram_tensor("aosp", [H * VD, S // 2], F32, kind="Internal").ap()
    hsp_d = nc.dram_tensor("hsp", [2, 128, DH], F32, kind="Internal").ap()
    t1c_d = nc.dram_tensor("t1c", [16, 128, DH], BF16, kind="Internal").ap()
    t2c_d = nc.dram_tensor("t2c", [16, 128, DH], BF16, kind="Internal").ap()
    t3c_d = nc.dram_tensor("t3c", [64, 128, D], BF16, kind="Internal").ap()
    wcache_d = {1: t1c_d, 2: t2c_d, 3: t3c_d}

    dbg = {}
    for spec in debug_dumps:
        name, shape = spec[0], spec[1]
        dt_ = spec[2] if len(spec) > 2 else F32
        dbg[name] = nc.dram_tensor("dbg_" + name, list(shape), dt_, kind="ExternalOutput").ap()

    kb = KB(nc)
    with contextlib.ExitStack() as stack:
        A32W, A16W = 45700, 14500
        a32_t = stack.enter_context(nc.sbuf_tensor("A32", [128, A32W], F32))
        a16_t = stack.enter_context(nc.sbuf_tensor("A16", [128, A16W], BF16))
        a32 = Arena(a32_t, A32W, "a32")
        a16 = Arena(a16_t, A16W, "a16")
        PB = [stack.enter_context(nc.psum_tensor(f"PB{i}", [128, 512], F32)) for i in range(8)]

        def dump(name, ap, after=()):
            if name not in dbg:
                return False
            for sem_name in after:
                kb.wait(Sy, sem_name)
            kb.dma(Sy, dbg[name], ap, "dD")
            return True

        a32.alloc("identf", 128)
        a32.alloc("onescol", 1)
        a32.alloc("onesrow", 128)
        a32.alloc("qpos", 16)
        a32.alloc("sc", 256)
        a32.alloc("invfb", 64)
        a32.alloc("onesm", 128)
        a32.alloc("cst", 8)
        CSTS = {}
        def cst(v):
            assert v in CSTS, v
            return a32.ap("cst", CSTS[v], CSTS[v] + 1)
        def reg_cst(v):
            if v not in CSTS:
                CSTS[v] = len(CSTS)
                kb.op(G, "memset", a32.ap("cst", CSTS[v], CSTS[v] + 1), float(v))
            return cst(v)
        a32.alloc("pcol", 16)
        a16.alloc("identb", 128)
        a32.alloc("acol", 4)
        a32.alloc("thcol", 2)
        a32.alloc("lncol", 40)
        a32.alloc("cosq", 2048)
        a32.alloc("sinq", 2048)
        a32.alloc("ckvT", 4 * 4096)
        a32.alloc("kp", 4096)
        a32.alloc("swap64", 64)
        IDF = a32.ap("identf")
        IDB = a16.ap("identb")
        ONESC = a32.ap("onescol")
        ONESR = a32.ap("onesrow", p0=0, p1=1)

        a32.alloc("cosk", 2048)
        a32.alloc("sink", 2048)
        a32.alloc("tabscr", 10240)

        def emit_round(y_ap, nf_ap, int_ap, b_ap, c_ap, vsem="v", gsem="gd"):
            kb.wait(G, vsem)
            kb.dma(G, int_ap, y_ap, gsem)
            kb.wait(G, gsem)
            kb.dma(G, nf_ap, int_ap, gsem)
            kb.wait(V, gsem)
            kb.op(V, "tensor_sub", out=b_ap, in0=y_ap, in1=nf_ap)
            kb.op(V, "tensor_single_scalar", out=c_ap, in_=b_ap, scalar=0.5, op=ALU.is_gt)
            kb.op(V, "tensor_add", out=nf_ap, in0=nf_ap, in1=c_ap)
            kb.op(V, "tensor_single_scalar", out=c_ap, in_=b_ap, scalar=-0.5, op=ALU.is_lt)
            kb.op(V, "tensor_sub", out=nf_ap, in0=nf_ap, in1=c_ap)

        def emit_sincos(src_ap, p0, p1, w, scr_name, cos_ap, sin_ap, tag):
            t = lambda lo, hi: a32.ap(scr_name, lo * w, hi * w, p0, p1)
            ii = a32.ap(scr_name, 3 * w, 4 * w, p0, p1).bitcast(I32)
            kb.op(V, "tensor_single_scalar", out=t(0, 1), in_=src_ap, scalar=INV2PI_HI, op=ALU.mult)
            kb.op(V, "scalar_tensor_tensor", out=t(0, 1), in0=src_ap, scalar=INV2PI_LO,
                  in1=t(0, 1), op0=ALU.mult, op1=ALU.add, inc=("v", 1))
            kb.wait(G, "v")
            kb.dma(G, ii, t(0, 1), "gd")
            kb.wait(G, "gd")
            kb.dma(G, t(1, 2), ii, "gd")
            kb.wait(V, "gd")
            kb.op(V, "tensor_sub", out=t(2, 3), in0=t(0, 1), in1=t(1, 2))
            kb.op(V, "tensor_single_scalar", out=t(3, 4), in_=t(2, 3), scalar=0.5, op=ALU.is_gt)
            kb.op(V, "tensor_add", out=t(1, 2), in0=t(1, 2), in1=t(3, 4))
            kb.op(V, "tensor_single_scalar", out=t(3, 4), in_=t(2, 3), scalar=-0.5, op=ALU.is_lt)
            kb.op(V, "tensor_sub", out=t(1, 2), in0=t(1, 2), in1=t(3, 4))
            kb.op(V, "scalar_tensor_tensor", out=t(0, 1), in0=t(1, 2), scalar=-TWOPI_HI,
                  in1=src_ap, op0=ALU.mult, op1=ALU.add)
            kb.op(V, "scalar_tensor_tensor", out=t(0, 1), in0=t(1, 2), scalar=-TWOPI_LO,
                  in1=t(0, 1), op0=ALU.mult, op1=ALU.add)
            kb.op(V, "scalar_tensor_tensor", out=t(1, 2), in0=t(0, 1), scalar=-1.0,
                  in1=t(0, 1), op0=ALU.mult, op1=ALU.max)
            kb.op(V, "tensor_scalar", out=t(1, 2), in0=t(1, 2), scalar1=-1.0, scalar2=PIHALF,
                  op0=ALU.mult, op1=ALU.add, inc=("v", 1))
            kb.wait(Sc, "v")
            kb.op(Sc, "activation", out=sin_ap, in_=t(0, 1), func=AF.Sin)
            kb.op(Sc, "activation", out=cos_ap, in_=t(1, 2), func=AF.Sin, inc=("s", 1))

        def emit_p0a():
            for v in (1e-8, 1e-5, 1e-6):
                reg_cst(v)
            kb.op(G, "memset", IDF, 1.0)
            kb.op(G, "affine_select", IDF, IDF, [[1, 128]], ALU.is_equal, 0.0,
                  base=0, channel_multiplier=-1)
            kb.op(G, "memset", ONESC, 1.0)
            kb.op(G, "memset", ONESR, 1.0)
            kb.op(G, "memset", a32.ap("onesm"), 1.0)
            kb.op(G, "memset", a32.ap("sc", 0, 1, 0, 64), 10000.0)
            kb.op(G, "iota", a32.ap("sc", 1, 2, 0, 32), [[1, 1]], base=0,
                  channel_multiplier=1, allow_small_or_imprecise_dtypes=True)
            kb.op(G, "iota", a32.ap("sc", 1, 2, 32, 64), [[1, 1]], base=0,
                  channel_multiplier=1, allow_small_or_imprecise_dtypes=True)
            kb.op(G, "iota", a32.ap("sc", 8, 8 + 64, 0, 1), [[0, 2], [1, 32]], base=0,
                  channel_multiplier=0, allow_small_or_imprecise_dtypes=True, inc=("gp", 1))
            kb.wait(Sc, "gp")
            kb.op(Sc, "activation", out=a32.ap("sc", 2, 3, 0, 64), in_=a32.ap("sc", 0, 1, 0, 64),
                  func=AF.Ln, inc=("s", 1))
            kb.wait(V, "s")
            kb.op(V, "tensor_single_scalar", out=a32.ap("sc", 3, 4, 0, 64),
                  in_=a32.ap("sc", 1, 2, 0, 64), scalar=a32.ap("sc", 2, 3, 0, 64), op=ALU.mult)
            kb.op(V, "tensor_single_scalar", out=a32.ap("sc", 8 + 64, 8 + 128, 0, 1),
                  in_=a32.ap("sc", 8, 8 + 64, 0, 1), scalar=a32.ap("sc", 2, 3, 0, 1),
                  op=ALU.mult, inc=("v", 1))
            kb.wait(Sc, "v")
            kb.op(Sc, "activation", out=a32.ap("sc", 4, 5, 0, 64), in_=a32.ap("sc", 3, 4, 0, 64),
                  func=AF.Exp, scale=-1.0 / 32.0)
            kb.op(Sc, "activation", out=a32.ap("sc", 8 + 128, 8 + 192, 0, 1),
                  in_=a32.ap("sc", 8 + 64, 8 + 128, 0, 1),
                  func=AF.Exp, scale=-1.0 / 32.0, inc=("s", 1))
            kb.wait(T, "s")
            kb.op(T, "matmul", PB[0][:, 0:64], ONESR, a32.ap("sc", 8 + 128, 8 + 192, 0, 1),
                  start=True, stop=True, inc=("t", 1))
            kb.wait(Sc, "t")
            kb.op(Sc, "copy", out=a32.ap("invfb"), in_=PB[0][:, 0:64], inc=("s", 1))
            kb.op(G, "iota", a32.ap("tabscr", 0, 32), [[128, 32]], base=0,
                  channel_multiplier=1, allow_small_or_imprecise_dtypes=True, inc=("gp", 1))
            kb.wait(V, "gp")
            kb.wait(V, "s")
            if dump("posb2", a32.ap("tabscr", 0, 32), after=("gp",)):
                kb.wait(V, "dD")
            for kt in range(32):
                kb.op(V, "tensor_single_scalar", out=a32.ap("cosk", kt * 64, kt * 64 + 64),
                      in_=a32.ap("invfb"), scalar=a32.ap("tabscr", kt, kt + 1),
                      op=ALU.mult)
            kb.op(V, "engine_nop", inc=("v", 1))
            if dump("freqs2", a32.ap("cosk"), after=("v",)):
                kb.wait(V, "dD")
                kb.wait(G, "dD")
            emit_sincos(a32.ap("cosk"), 0, 128, 2048, "tabscr", a32.ap("cosk"),
                        a32.ap("sink"), "k")
            kb.wait(Sy, "s")
            kb.dma(Sy, a32.ap("tabscr", 0, 2048, 0, 64), qposf_d, "dA")
            kb.wait(V, "dA")
            kb.op(V, "tensor_single_scalar", out=a32.ap("cosq", 0, 2048, 0, 64),
                  in_=a32.ap("tabscr", 0, 2048, 0, 64), scalar=a32.ap("sc", 4, 5, 0, 64),
                  op=ALU.mult, inc=("v", 1))
            emit_sincos(a32.ap("cosq", 0, 2048, 0, 64), 0, 64, 2048, "tabscr",
                        a32.ap("cosq", 0, 2048, 0, 64), a32.ap("sinq", 0, 2048, 0, 64), "q")
            kb.wait(V, "s")
            kb.op(V, "tensor_single_scalar", out=a32.ap("sinq", 0, 2048, 0, 32),
                  in_=a32.ap("sinq", 0, 2048, 0, 32), scalar=-1.0, op=ALU.mult,
                  inc=("v", 1))
            kb.op(Sc, "copy", out=IDB, in_=IDF, inc=("s", 1))
            kb.op(G, "memset", a32.ap("swap64", 0, 64, 0, 64), 0.0, inc=("gp", 1))
            kb.wait(Sc, "gp")
            kb.op(Sc, "copy", out=a32.ap("swap64", 32, 64, 0, 32),
                  in_=IDF[0:32, 0:32])
            kb.op(Sc, "copy", out=a32.ap("swap64", 0, 32, 32, 64),
                  in_=IDF[32:64, 32:64], inc=("s", 1))
            kb.dma(Sy, a32.ap("qpos"), qpos_d, "dA")
            dump("cosk", a32.ap("cosk"), after=("s",))
            dump("sink", a32.ap("sink"), after=("s",))
            dump("cosq", a32.ap("cosq", 0, 2048, 0, 64), after=("s", "v"))
            dump("sinq", a32.ap("sinq", 0, 2048, 0, 64), after=("s", "v"))
            dump("invfb", a32.ap("invfb"), after=("s",))

        if "P0a" in phases:
            emit_p0a()
            kb.barrier()
            a32.free("sc", "invfb")
            dump("coskA", a32.ap("cosk"), after=("s", "v", "gd"))
            if "coskA" in dbg:
                kb.wait(V, "dD")
                kb.wait(Sc, "dD")
                kb.wait(G, "dD")
                kb.wait(T, "dD")
            a32.free("tabscr")


        W_SPECS = {1: (w1T_d, 16, DH), 2: (w2T_d, 16, DH), 3: (w3T_d, 64, D)}

        def emit_p0b_weight(wi):
            wd, nchunks, cw = W_SPECS[wi]
            a32.alloc("wbuf", 2 * 4096)
            a32.alloc("accvec", 192)
            a16.alloc("tb1", 4096)
            a16.alloc("tb2", 4096)
            hw = min(cw, 4096)
            nh = cw // hw
            units = [(i, hc) for i in range(nchunks) for hc in range(nh)]
            wb = lambda par: a32.ap("wbuf", par * 4096, par * 4096 + hw)
            vC = []
            for u, (i, hc) in enumerate(units):
                if u >= 2:
                    kb.wait(Sy, "v", vC[u - 2])
                kb.dma(Sy, wb(u % 2), wd[i * 128:(i + 1) * 128, hc * hw:(hc + 1) * hw], "dA")
                kb.wait(V, "dA")
                vC.append(kb.op(V, "reduce_sum", out=a32.ap("accvec", u, u + 1),
                                in_=wb(u % 2), axis=AX.X, apply_absolute_value=True,
                                inc=("v", 1)))
            kb.wait(Sc, "v")
            kb.op(Sc, "activation", out=a32.ap("accvec", 96, 96 + len(units)),
                  in_=a32.ap("accvec", 0, len(units)), func=AF.Identity,
                  accum_out=a32.ap("pcol", 0, 1), inc=("s", 1))
            kb.wait(T, "s")
            kb.wait(T, "v")
            kb.op(T, "matmul", PB[6][:, 0:1], a32.ap("onesm"), a32.ap("pcol", 0, 1),
                  start=True, stop=True, inc=("t", 1))
            kb.wait(V, "t")
            PC = lambda c: a32.ap("pcol", c, c + 1)
            s_, seps, u_, vv, lnu, lnv, dd = (PC(1), PC(2), PC(3), PC(4), PC(5), PC(6), PC(7))
            ath = PC(9)
            a_ = a32.ap("acol", wi, wi + 1)
            kb.op(V, "tensor_single_scalar", out=s_, in_=PB[6][:, 0:1],
                  scalar=float(1.0 / (1 << 24)), op=ALU.mult, inc=("v", 1))
            kb.wait(Sc, "v")
            kb.op(Sc, "activation", out=seps, in_=s_, func=AF.Identity, bias=cst(1e-8))
            kb.op(Sc, "activation", out=u_, in_=s_, func=AF.Identity, bias=ONESC)
            kb.op(Sc, "activation", out=vv, in_=s_, func=AF.Identity, scale=-1.0,
                  bias=ONESC)
            kb.op(Sc, "activation", out=lnu, in_=u_, func=AF.Ln)
            kb.op(Sc, "activation", out=lnv, in_=vv, func=AF.Ln, inc=("s", 1))
            kb.wait(V, "s")
            kb.op(V, "tensor_sub", out=dd, in0=lnu, in1=lnv)
            kb.op(V, "tensor_tensor", out=PC(11), in0=seps, in1=ath, op=ALU.mult,
                  inc=("v", 1))
            kb.wait(Sc, "v")
            kb.op(Sc, "activation", out=a_, in_=dd, func=AF.Identity, scale=0.5)
            kb.op(Sc, "copy", out=a32.ap("thcol", 0, 1), in_=PC(11))
            kb.op(Sc, "activation", out=a32.ap("thcol", 1, 2), in_=PC(11),
                  func=AF.Identity, scale=-1.0, inc=("s", 1))
            kb.wait(V, "s")
            if dump(f"th{wi}", a32.ap("thcol", 0, 2), after=("s",)):
                kb.wait(V, "dD")
                kb.wait(Sc, "dD")
                kb.wait(T, "dD")
            cache = wcache_d[wi]
            vC2 = []
            gC2 = []
            for u, (i, hc) in enumerate(units):
                if u >= 2:
                    kb.wait(Sy, "v", vC2[u - 2])
                kb.dma(Sy, wb(u % 2), wd[i * 128:(i + 1) * 128, hc * hw:(hc + 1) * hw], "dA")
                kb.wait(V, "dA")
                if gC2:
                    kb.wait(V, "g", gC2[-1])
                wsl = wb(u % 2)
                kb.op(V, "tensor_scalar", out=a16.ap("tb1", 0, hw), in0=wsl,
                      scalar1=a32.ap("thcol", 0, 1), scalar2=None, op0=ALU.is_ge)
                kb.op(V, "tensor_scalar", out=a16.ap("tb2", 0, hw), in0=wsl,
                      scalar1=a32.ap("thcol", 1, 2), scalar2=-1.0,
                      op0=ALU.is_le, op1=ALU.mult)
                last = kb.op(V, "tensor_add", out=a16.ap("tb1", 0, hw),
                             in0=a16.ap("tb1", 0, hw), in1=a16.ap("tb2", 0, hw),
                             inc=("v", 1))
                kb.wait(G, "v")
                gC2.append(kb.dma(G, cache[i, :, hc * hw:(hc + 1) * hw],
                                  a16.ap("tb1", 0, hw), "g"))
                vC2.append(last)
            kb.wait(Sy, "g")
            a32.free("wbuf", "accvec")
            a16.free("tb1", "tb2")

        def emit_p0b():
            kb.op(G, "memset", a32.ap("pcol", 8, 9), 3.0, inc=("gp", 1))
            kb.wait(Sc, "gp")
            kb.op(Sc, "activation", out=a32.ap("pcol", 10, 11), in_=a32.ap("pcol", 8, 9),
                  func=AF.Ln, inc=("s", 1))
            kb.wait(V, "s")
            kb.op(V, "tensor_single_scalar", out=a32.ap("pcol", 9, 10),
                  in_=a32.ap("pcol", 10, 11), scalar=0.5, op=ALU.mult, inc=("v", 1))
            for wi in (1, 2, 3):
                emit_p0b_weight(wi)
            dump("svals", a32.ap("pcol", 0, 16, 0, 1), after=("v", "s"))
            dump("acols", a32.ap("acol"), after=("s",))
            if "t1probe" in dbg:
                a16.alloc("probe", 8192)
                kb.wait(Sy, "g")
                kb.dma(Sy, a16.ap("probe"), t1c_d[3], "dA")
                kb.wait(Sy, "dA")
                kb.dma(Sy, dbg["t1probe"], a16.ap("probe"), "dD")
                kb.dma(Sy, a16.ap("probe", 0, 2048), t3c_d[5], "dA")
                kb.wait(Sy, "dA")
                kb.dma(Sy, dbg["t3probe"], a16.ap("probe", 0, 2048), "dD")
                kb.wait(Sy, "dD")
                a16.free("probe")

        if "P0b" in phases:
            emit_p0b()
            kb.barrier()
            dump("coskB", a32.ap("cosk"), after=("s", "v", "g"))
            if "coskB" in dbg:
                kb.wait(V, "dD")
                kb.wait(Sc, "dD")
                kb.wait(G, "dD")
                kb.wait(T, "dD")
                kb.wait(Sy, "dD")


        def LC(c, n=1):
            return a32.ap("lncol", c, c + n)

        TRT = {"n": 0, "ev": []}

        def transpose_evac(src_aps, dst_aps, ident, pbanks=(4, 5)):
            kb.wait(T, "v")
            for i, (sap, dap) in enumerate(zip(src_aps, dst_aps)):
                bank = PB[pbanks[i % 2]]
                w = sap.shape[-1]
                if TRT["n"] >= 2:
                    kb.wait(T, "s", TRT["ev"][TRT["n"] - 2])
                kb.op(T, "transpose", bank[0:w, 0:128], sap, ident, inc=("t", 1))
                kb.wait(Sc, "t")
                TRT["ev"].append(kb.op(Sc, "copy", out=dap, in_=bank[0:w, 0:128],
                                       inc=("s", 1)))
                TRT["n"] += 1

        def emit_ln_super(x_aps, out_aps, w, eps, scr_ap, cb=0):
            n = len(x_aps)
            s1, s2, m, e2, nmsq, nme, var, sd, inv = (LC(cb + 4 * i, n) for i in range(9))
            for i, xa in enumerate(x_aps):
                kb.op(V, "reduce_sum", out=LC(cb + 0 * 4 + i), in_=xa, axis=AX.X,
                      inc=("v", 1) if i == n - 1 else None)
            kb.wait(Sc, "v")
            for i, xa in enumerate(x_aps):
                kb.op(Sc, "activation", out=scr_ap, in_=xa, func=AF.Square,
                      accum_out=LC(cb + 4 + i))
            kb.op(Sc, "activation", out=m, in_=s1, func=AF.Identity,
                  scale=1.0 / w, inc=("s", 1))
            kb.wait(V, "s")
            kb.op(V, "tensor_single_scalar", out=e2, in_=s2, scalar=1.0 / w, op=ALU.mult)
            kb.op(V, "scalar_tensor_tensor", out=nmsq, in0=m, scalar=-1.0, in1=m,
                  op0=ALU.mult, op1=ALU.mult, inc=("v", 1))
            kb.wait(Sc, "v")
            kb.op(Sc, "activation", out=nme, in_=nmsq, func=AF.Identity,
                  bias=cst(eps), inc=("s", 1))
            kb.wait(V, "s")
            kb.op(V, "tensor_add", out=var, in0=e2, in1=nme, inc=("v", 1))
            kb.wait(Sc, "v")
            kb.op(Sc, "activation", out=sd, in_=var, func=AF.Sqrt, inc=("s", 1))
            kb.wait(V, "s")
            kb.op(V, "reciprocal", out=inv, in_=sd)
            for i, xa in enumerate(x_aps):
                kb.op(V, "tensor_single_scalar", out=out_aps[i], in_=xa,
                      scalar=LC(cb + 8 + i), op=ALU.subtract,
                      inc=("v", 1) if i == n - 1 else None)
            kb.wait(Sc, "v")
            for i in range(n):
                kb.op(Sc, "activation", out=out_aps[i], in_=out_aps[i], func=AF.Identity,
                      scale=LC(cb + 32 + i), inc=("s", 1) if i == n - 1 else None)
            kb.wait(V, "s")


        TCT = {"kva": []}

        def emit_p1():
            a32.alloc("wbcast", 512)
            a32.alloc("xts", 4 * 2048)
            a32.alloc("lnscr", 2048)
            a32.alloc("lnxTt", 4 * 128)
            a32.alloc("waTc", 2 * 576)
            a32.alloc("ckscr", 512)
            a32.alloc("cktmp", 512)
            kb.dma(Sy, a32.ap("ckscr", 0, 512, 0, 1), kvnw_d, "dA")
            kb.wait(T, "dA")
            kb.op(T, "matmul", PB[6][:, 0:512], ONESR, a32.ap("ckscr", 0, 512, 0, 1),
                  start=True, stop=True, inc=("t", 1))
            kb.wait(Sc, "t")
            kb.op(Sc, "copy", out=a32.ap("wbcast"), in_=PB[6][:, 0:512], inc=("s", 1))

            xap = lambda sub: a32.ap("xts", sub * 2048, (sub + 1) * 2048)
            for st in range(8):
                if st > 0:
                    kb.wait(Sy, "t")
                for sub in range(4):
                    r0 = st * 512 + sub * 128
                    kb.dma(Sy, xap(sub), xfull_d[r0:r0 + 128, :], "dA")
                kb.wait(V, "dA")
                emit_ln_super([xap(i) for i in range(4)], [xap(i) for i in range(4)],
                              D, 1e-5, a32.ap("lnscr"))
                for sub in range(4):
                    tile = st * 4 + sub
                    kb.wait(T, "s")
                    kb.wait(T, "v")
                    for ic in range(16):
                        par = ic % 2
                        lxT = a32.ap("lnxTt", par * 128, (par + 1) * 128)
                        wpar = ic % 2
                        wc = a32.ap("waTc", wpar * 576, wpar * 576 + 576)
                        g = len(TCT["kva"])
                        if g >= 2:
                            kb.wait(Sy, "t", TCT["kva"][g - 2])
                        kb.dma(Sy, wc, waT_d[ic * 128:(ic + 1) * 128, :], f"dA{wpar}")
                        if ic >= 2:
                            kb.wait(T, "s")
                        kb.op(T, "transpose", PB[4 + par][0:128, 0:128],
                              xap(sub)[:, ic * 128:(ic + 1) * 128], IDF, inc=("t", 1))
                        kb.wait(Sc, "t")
                        kb.op(Sc, "copy", out=lxT, in_=PB[4 + par][0:128, 0:128],
                              inc=("s", 1))
                        kb.wait(T, "s")
                        kb.wait(T, f"dA{wpar}")
                        kb.op(T, "matmul", PB[0][:, 0:512], lxT, wc[:, 0:512],
                              start=(ic == 0), stop=(ic == 15), inc=("t", 1))
                        t1 = kb.op(T, "matmul", PB[1][:, 0:64], lxT, wc[:, 512:576],
                                   start=(ic == 0), stop=(ic == 15), inc=("t", 1))
                        TCT["kva"].append(t1)
                    kb.wait(Sc, "t")
                    kb.op(Sc, "activation", out=a32.ap("ckscr"), in_=PB[0][:, 0:512],
                          func=AF.Square, accum_out=LC(36), inc=("s", 1))
                    kb.wait(V, "s")
                    kb.op(V, "tensor_single_scalar", out=LC(37), in_=LC(36),
                          scalar=1.0 / RANK, op=ALU.mult, inc=("v", 1))
                    kb.wait(Sc, "v")
                    kb.op(Sc, "activation", out=LC(38), in_=LC(37), func=AF.Sqrt,
                          bias=cst(1e-6), inc=("s", 1))
                    kb.wait(V, "s")
                    kb.op(V, "reciprocal", out=LC(39), in_=LC(38), inc=("v", 1))
                    kb.wait(Sc, "v")
                    kb.op(Sc, "activation", out=a32.ap("ckscr"), in_=PB[0][:, 0:512],
                          func=AF.Identity, scale=LC(39), inc=("s", 1))
                    kb.wait(V, "s")
                    kb.op(V, "tensor_mul", out=a32.ap("ckscr"), in0=a32.ap("ckscr"),
                          in1=a32.ap("wbcast"))
                    ck = lambda lo, hi: a32.ap("cosk", tile * 64 + lo, tile * 64 + hi)
                    sk = lambda lo, hi: a32.ap("sink", tile * 64 + lo, tile * 64 + hi)
                    kt = a32.ap("cktmp", 0, 64)
                    kb.op(V, "tensor_mul", out=kt[:, 0:32], in0=PB[1][:, 0:32], in1=ck(0, 32))
                    kb.op(V, "tensor_mul", out=kt[:, 32:64], in0=PB[1][:, 32:64], in1=ck(32, 64))
                    kb.op(V, "tensor_mul", out=a32.ap("cktmp", 64, 96), in0=PB[1][:, 32:64],
                          in1=sk(0, 32))
                    kb.op(V, "tensor_mul", out=a32.ap("cktmp", 96, 128), in0=PB[1][:, 0:32],
                          in1=sk(32, 64))
                    kb.op(V, "tensor_sub", out=kt[:, 0:32], in0=kt[:, 0:32],
                          in1=a32.ap("cktmp", 64, 96))
                    kb.op(V, "tensor_add", out=kt[:, 32:64], in0=kt[:, 32:64],
                          in1=a32.ap("cktmp", 96, 128), inc=("v", 1))
                    if st == 0 and sub == 0 and "kt0" in dbg:
                        dump("coskP1", a32.ap("cosk"), after=("s",))
                        dump("sinkP1", a32.ap("sink"), after=("s",))
                        kb.wait(Sc, "v")
                        kb.op(Sc, "copy", out=a32.ap("lnscr", 0, 64), in_=a32.ap("cktmp", 0, 64))
                        kb.op(Sc, "copy", out=a32.ap("lnscr", 64, 128), in_=PB[1][:, 0:64],
                              inc=("s", 1))
                        dump("kt0", a32.ap("lnscr", 0, 64), after=("s",))
                        dump("kpe0", a32.ap("lnscr", 64, 128), after=("s",))
                        kb.wait(V, "dD")
                        kb.wait(T, "dD")
                    tok0 = st * 512 + sub * 128
                    transpose_evac(
                        [a32.ap("ckscr", rc * 128, (rc + 1) * 128) for rc in range(4)] + [kt],
                        [a32.ap("ckvT", rc * 4096 + tok0, rc * 4096 + tok0 + 128)
                         for rc in range(4)] + [a32.ap("kp", tok0, tok0 + 128, 0, 64)],
                        IDF)
            a32.free("wbcast", "xts", "lnscr", "lnxTt", "waTc", "ckscr", "cktmp")

        if "P1" in phases:
            emit_p1()
            kb.barrier()
            dump("ckvT", a32.ap("ckvT"), after=("s", "v"))
            dump("kpeT", a32.ap("kp", 0, 4096, 0, 64), after=("s", "v"))

        def emit_p2():
            a32.free("cosk", "sink")
            a32.alloc("xts2", 4 * 2048)
            a32.alloc("lnscr2", 2048)
            a32.alloc("lnxT", 16 * 512)
            a32.alloc("wqc", 2 * 384)
            a32.alloc("qstg", 512)
            xap = lambda sub: a32.ap("xts2", sub * 2048, (sub + 1) * 2048)
            qmm = []
            for st in range(4):
                kb.wait(Sy, "t")
                kb.wait(Sy, "s")
                for sub in range(4):
                    r0 = st * 512 + sub * 128
                    kb.dma(Sy, xap(sub), xown_d[r0:r0 + 128, :], "dA")
                kb.wait(V, "dA")
                emit_ln_super([xap(i) for i in range(4)], [xap(i) for i in range(4)],
                              D, 1e-5, a32.ap("lnscr2"))
                srcs, dsts = [], []
                for ic in range(16):
                    for sub in range(4):
                        srcs.append(xap(sub)[:, ic * 128:(ic + 1) * 128])
                        dsts.append(a32.ap("lnxT", ic * 512 + sub * 128,
                                           ic * 512 + sub * 128 + 128))
                transpose_evac(srcs, dsts, IDF)
                for grp in range(8):
                    h0, h1 = 2 * grp, 2 * grp + 1
                    kb.wait(T, "s")
                    kb.wait(T, "v")
                    for ic in range(16):
                        par = ic % 2
                        wc = a32.ap("wqc", par * 384, (par + 1) * 384)
                        g = len(qmm)
                        if g >= 2:
                            kb.wait(Sy, "t", qmm[g - 2])
                        kb.dma(Sy, wc, wqT_d[ic * 128:(ic + 1) * 128,
                                             grp * 384:(grp + 1) * 384], f"dA{par}")
                        kb.wait(T, f"dA{par}")
                        kb.op(T, "matmul", PB[0][:, 0:512], wc[:, 0:128],
                              a32.ap("lnxT", ic * 512, (ic + 1) * 512),
                              start=(ic == 0), stop=(ic == 15), inc=("t", 1))
                        kb.op(T, "matmul", PB[2][0:64, 0:512], wc[:, 128:192],
                              a32.ap("lnxT", ic * 512, (ic + 1) * 512),
                              start=(ic == 0), stop=(ic == 15), inc=("t", 1))
                        kb.op(T, "matmul", PB[1][:, 0:512], wc[:, 192:320],
                              a32.ap("lnxT", ic * 512, (ic + 1) * 512),
                              start=(ic == 0), stop=(ic == 15), inc=("t", 1))
                        t1 = kb.op(T, "matmul", PB[3][0:64, 0:512], wc[:, 320:384],
                                   a32.ap("lnxT", ic * 512, (ic + 1) * 512),
                                   start=(ic == 0), stop=(ic == 15), inc=("t", 1))
                        qmm.append(t1)
                    kb.wait(Sc, "t")
                    kb.wait(V, "t")
                    cq = lambda r0, r1: a32.ap("cosq", st * 512, st * 512 + 512, r0, r1)
                    sq = lambda r0, r1: a32.ap("sinq", st * 512, st * 512 + 512, r0, r1)
                    for hi, (pbn, pbr) in enumerate(((0, 2), (1, 3))):
                        h = h0 if hi == 0 else h1
                        kb.wait(Sc, "dA")
                        kb.wait(V, "dA")
                        qs = a32.ap("lnscr2", 0, 512)
                        qr = a32.ap("qstg", 0, 512, 0, 64)
                        qpe = a32.ap("lnscr2", 512, 1024, 0, 64)
                        qsw = a32.ap("lnscr2", 1024, 1536)
                        qt = a32.ap("lnscr2", 1536, 2048, 0, 64)
                        kb.op(Sc, "copy", out=qs, in_=PB[pbn][:, 0:512])
                        kb.op(Sc, "copy", out=qpe, in_=PB[pbr][0:64, 0:512], inc=("s", 1))
                        kb.wait(T, "s")
                        kb.wait(T, "v")
                        kb.op(T, "matmul", PB[4][0:64, 0:512],
                              a32.ap("swap64", 0, 64, 0, 64), qpe,
                              start=True, stop=True, inc=("t", 1))
                        kb.wait(V, "t")
                        kb.op(V, "tensor_mul", out=qr, in0=PB[pbr][0:64, 0:512],
                              in1=cq(0, 64))
                        kb.op(V, "tensor_mul", out=qt, in0=PB[4][0:64, 0:512],
                              in1=sq(0, 64))
                        kb.op(V, "tensor_add", out=qr, in0=qr, in1=qt, inc=("v", 1))
                        if st == 0 and grp == 0 and hi == 0 and "qr0" in dbg:
                            dump("qr0", a32.ap("qstg", 0, 512), after=("v",))
                            kb.op(Sc, "copy", out=a32.ap("lnscr2", 512, 1024, 0, 64),
                                  in_=PB[pbr][0:64, 0:512], inc=("s", 1))
                            dump("qpe0", a32.ap("lnscr2", 512, 1024, 0, 64), after=("s",))
                            kb.wait(V, "dD")
                            kb.wait(T, "dD")
                            kb.wait(Sc, "dD")
                        kb.wait(Sy, "s")
                        kb.wait(Sy, "v")
                        kb.dma(Sy, qsp_d[h * QHD:h * QHD + 128, st * 512:(st + 1) * 512],
                               qs, "dA")
                        kb.dma(Sy, qsp_d[h * QHD + 128:h * QHD + 192,
                                         st * 512:(st + 1) * 512], qr, "dA")
            a32.free("xts2", "lnscr2", "lnxT", "wqc", "qstg")

        if "P2" in phases:
            emit_p2()
            kb.barrier()

        def emit_p3():
            a32.free("cosq", "sinq")
            a32.alloc("qro", 2048)
            a32.alloc("knT", 4096)
            a32.alloc("vsb", 4096)
            a32.alloc("qn", 2048)
            a32.alloc("pbuf", 2 * 4096)
            a32.alloc("aoT", 2048)
            a32.alloc("wkb", 1024)
            a32.alloc("msk", 512)
            a32.alloc("ptb0", 256)
            a32.alloc("ptb1", 256)
            for h in range(p3_heads):
                kb.wait(Sy, "t")
                kb.wait(Sy, "s")
                kb.wait(Sy, "v")
                kb.wait(Sc, "dB")
                kb.dma(Sy, a32.ap("qn"), qsp_d[h * QHD:h * QHD + 128, :], "dB")
                kb.dma(Sy, a32.ap("qro", 0, 2048, 0, 64),
                       qsp_d[h * QHD + 128:h * QHD + 192, :], "dB")
                for rc in range(4):
                    kb.dma(Sy, a32.ap("wkb", rc * 256, (rc + 1) * 256),
                           wkvbT_d[rc * 128:(rc + 1) * 128,
                                   h * 256:(h + 1) * 256], "dB")
                kb.wait(T, "dB")
                knev = []
                for kt in range(8):
                    pb = PB[kt % 2]
                    if kt >= 2:
                        kb.wait(T, "s", knev[kt - 2])
                    for rc in range(4):
                        kb.op(T, "matmul", pb[:, 0:512],
                              a32.ap("wkb", rc * 256, rc * 256 + 128),
                              a32.ap("ckvT", rc * 4096 + kt * 512, rc * 4096 + (kt + 1) * 512),
                              start=(rc == 0), stop=(rc == 3), inc=("t", 1))
                    kb.wait(Sc, "t")
                    knev.append(kb.op(Sc, "copy", out=a32.ap("knT", kt * 512, (kt + 1) * 512),
                          in_=pb[:, 0:512], inc=("s", 1)))
                vev = []
                for ktc in range(32):
                    pb = PB[2 + ktc % 2]
                    if ktc >= 2:
                        kb.wait(T, "s", vev[ktc - 2])
                    for rc in range(4):
                        kb.op(T, "matmul", pb[0:128, 0:128],
                              a32.ap("ckvT", rc * 4096 + ktc * 128, rc * 4096 + (ktc + 1) * 128),
                              a32.ap("wkb", rc * 256 + 128, (rc + 1) * 256),
                              start=(rc == 0), stop=(rc == 3), inc=("t", 1))
                    kb.wait(Sc, "t")
                    vev.append(kb.op(Sc, "copy", out=a32.ap("vsb", ktc * 128, (ktc + 1) * 128),
                          in_=pb[0:128, 0:128], inc=("s", 1)))
                for j in range(8):
                    kext = (j + 1) * 512
                    for ss in range(2):
                        qc0 = j * 256 + ss * 128
                        kb.wait(G, "v")
                        kb.wait(G, "s")
                        kb.op(G, "iota", a32.ap("msk"),
                              [[1, 512]], base=j * 512, channel_multiplier=0,
                              allow_small_or_imprecise_dtypes=True, inc=("gp", 1))
                        kb.wait(V, "gp")
                        kb.op(V, "tensor_scalar", out=a32.ap("msk"),
                              in0=a32.ap("msk"),
                              scalar1=a32.ap("qpos", 2 * j + ss, 2 * j + ss + 1),
                              scalar2=NEG, op0=ALU.is_gt, op1=ALU.mult, inc=("v", 1))
                        kb.wait(T, "s")
                        kb.wait(T, "v")
                        scev = []
                        for kt in range(j + 1):
                            pb = PB[kt % 2]
                            if kt >= 2:
                                sem_nm, val = scev[kt - 2]
                                kb.wait(T, sem_nm, val)
                            kb.op(T, "matmul", pb[:, 0:512],
                                  a32.ap("qn")[:, qc0:qc0 + 128],
                                  a32.ap("knT", kt * 512, (kt + 1) * 512),
                                  start=True, stop=False, inc=("t", 1))
                            kb.op(T, "matmul", pb[:, 0:512],
                                  a32.ap("qro", qc0, qc0 + 128, 0, 64),
                                  a32.ap("kp", kt * 512, (kt + 1) * 512, 0, 64),
                                  start=False, stop=True, inc=("t", 1))
                            dst = a32.ap("pbuf", ss * 4096 + kt * 512,
                                         ss * 4096 + (kt + 1) * 512)
                            if kt == j:
                                kb.wait(V, "t")
                                scev.append(("v", kb.op(V, "tensor_add", out=dst,
                                      in0=pb[:, 0:512],
                                      in1=a32.ap("msk"), inc=("v", 1))))
                            else:
                                kb.wait(Sc, "t")
                                scev.append(("s", kb.op(Sc, "copy", out=dst,
                                      in_=pb[:, 0:512], inc=("s", 1))))
                        P = a32.ap("pbuf", ss * 4096, ss * 4096 + kext)
                        kb.wait(V, "s")
                        kb.op(V, "reduce_max", out=LC(36), in_=P, axis=AX.X, inc=("v", 1))
                        kb.wait(Sc, "v")
                        kb.op(Sc, "copy", out=LC(35), in_=LC(36), inc=("s", 1))
                        kb.wait(V, "s")
                        kb.op(V, "tensor_single_scalar", out=LC(37), in_=LC(35),
                              scalar=-SCALE, op=ALU.mult, inc=("v", 1))
                        kb.wait(Sc, "v")
                        kb.op(Sc, "activation", out=P, in_=P, func=AF.Exp, scale=SCALE,
                              bias=LC(37), accum_out=LC(38), inc=("s", 1))
                        kb.wait(V, "s")
                        kb.op(V, "reciprocal", out=LC(39), in_=LC(38), inc=("v", 1))
                        kb.wait(Sc, "v")
                        kb.op(Sc, "activation", out=P, in_=P, func=AF.Identity,
                              scale=LC(39), inc=("s", 1))
                    kb.wait(T, "s")
                    nk = (j + 1) * 4
                    ptev = []
                    pvmm = []
                    for kc in range(nk):
                        kpar = kc % 2
                        for ss in range(2):
                            bank = PB[4 + (2 * kc + ss) % 2]
                            if 2 * kc + ss >= 2:
                                kb.wait(T, "s", ptev[2 * kc + ss - 2])
                            if ss == 0 and kc >= 2:
                                kb.wait(T, "t", pvmm[kc - 2])
                            kb.op(T, "transpose", bank[0:128, 0:128],
                                  a32.ap("pbuf", ss * 4096 + kc * 128,
                                         ss * 4096 + (kc + 1) * 128), IDF,
                                  inc=("t", 1))
                            kb.wait(Sc, "t")
                            if ss == 0 and kc >= 2:
                                kb.wait(Sc, "t", pvmm[kc - 2])
                            ptev.append(kb.op(Sc, "copy",
                                  out=a32.ap(f"ptb{kpar}", ss * 128, (ss + 1) * 128),
                                  in_=bank[0:128, 0:128], inc=("s", 1)))
                        kb.wait(T, "s", ptev[-1])
                        pvmm.append(kb.op(T, "matmul", PB[6][:, 0:256],
                              a32.ap("vsb", kc * 128, (kc + 1) * 128),
                              a32.ap(f"ptb{kpar}"),
                              start=(kc == 0), stop=(kc == nk - 1), inc=("t", 1)))
                    kb.wait(Sc, "t")
                    kb.op(Sc, "copy", out=a32.ap("aoT", j * 256, (j + 1) * 256),
                          in_=PB[6][:, 0:256], inc=("s", 1))
                kb.wait(Sy, "s")
                kb.dma(Sy, aosp_d[h * VD:(h + 1) * VD, :], a32.ap("aoT"), "dB")
            kb.barrier()
            a32.free("qro", "knT", "vsb", "qn", "pbuf", "ptb0", "ptb1", "aoT", "wkb", "msk")

        if "P3" in phases:
            emit_p3()
        def emit_quant_cols(src_ap, w, g_ap, inv_col, out_bf, tag, dbg_tag=None):
            rb = lambda i: a32.ap("rscr", i * 2048, i * 2048 + w)
            ii = a32.ap("rscr", 1 * 2048, 1 * 2048 + w).bitcast(I32)
            kb.wait(Sc, "v")
            kb.op(Sc, "activation", out=rb(0), in_=src_ap, func=AF.Identity,
                  scale=inv_col, inc=("s", 1))
            kb.wait(V, "s")
            kb.op(V, "tensor_mul", out=rb(0), in0=rb(0), in1=g_ap)
            kb.op(V, "reduce_max", out=LC(20), in_=rb(0), axis=AX.X,
                  apply_absolute_value=True, inc=("v", 1))
            kb.wait(Sc, "v")
            kb.op(Sc, "copy", out=LC(21), in_=LC(20), inc=("s", 1))
            kb.wait(V, "s")
            kb.op(V, "tensor_single_scalar", out=LC(22), in_=LC(21), scalar=1e-4,
                  op=ALU.max, inc=("v", 1))
            kb.wait(Sc, "v")
            kb.op(Sc, "copy", out=LC(23), in_=LC(22), inc=("s", 1))
            kb.wait(V, "s")
            kb.op(V, "reciprocal", out=LC(24), in_=LC(23), inc=("v", 1))
            kb.wait(Sc, "v")
            kb.op(Sc, "copy", out=LC(34), in_=LC(24), inc=("s", 1))
            kb.wait(V, "s")
            kb.op(V, "tensor_single_scalar", out=LC(25), in_=LC(34), scalar=127.0,
                  op=ALU.mult, inc=("v", 1))
            kb.wait(Sc, "v")
            kb.op(Sc, "activation", out=rb(0), in_=rb(0), func=AF.Identity,
                  scale=LC(25), inc=("s", 1))
            kb.wait(V, "s")
            if dbg_tag and f"{dbg_tag}_y" in dbg:
                dump(f"{dbg_tag}_y", rb(0), after=("s",))
                dump(f"{dbg_tag}_am", a32.ap("lncol", 16, 34), after=("s", "v"))
                kb.wait(V, "dD")
                kb.wait(Sc, "dD")
                kb.wait(G, "dD")
            kb.op(V, "engine_nop", inc=("v", 1))
            emit_round(rb(0), rb(2), ii, rb(1), a32.ap("aoc", 0, w), vsem="v", gsem="gd")
            kb.op(V, "engine_nop", inc=("v", 1))
            kb.wait(Sc, "v")
            kb.wait(Sc, "t")
            kb.op(Sc, "copy", out=out_bf, in_=rb(2), inc=("s", 1))
            return a32.ap("lncol", 23, 24)

        EMM = []
        Y3MM = []

        def emit_p4():
            a32.free("ckvT", "kp")
            a32.alloc("aoc", 4096)
            a32.alloc("woc", 2 * 2048)
            a32.alloc("resid", 2 * 2048)
            a32.alloc("ln2", 2 * 2048)
            a32.alloc("rscr", 3 * 2048)
            a32.alloc("g1b", 2048)
            a32.alloc("g2b", 2048)
            a32.alloc("g3b", 8192)
            a32.alloc("outacc", 2 * 2048)
            a32.alloc("grow", 2048)
            a16.alloc("kT1", 4096)
            a16.alloc("kT2", 4096)
            a16.alloc("kbf", 2048)
            a16.alloc("k3Ts", 2048)
            a32.alloc("tstrf", 2 * 1536)
            def tstr(lo, hi):
                return a32.ap("tstrf", lo // 2, hi // 2).bitcast(BF16)
            for gd_, dst, w in ((g1_d, "g1b", D), (g2_d, "g2b", D), (g3_d, "g3b", DH)):
                for seg in range(w // 2048):
                    kb.wait(Sy, "s")
                    kb.wait(Sy, "t")
                    kb.dma(Sy, a32.ap("grow", 0, 2048, 0, 1),
                           gd_[0:1, seg * 2048:(seg + 1) * 2048], "dC")
                    kb.wait(T, "dC")
                    for c in range(4):
                        kb.wait(T, "s")
                        kb.op(T, "matmul", PB[c % 2][:, 0:512], ONESR,
                              a32.ap("grow", c * 512, (c + 1) * 512, 0, 1),
                              start=True, stop=True, inc=("t", 1))
                        kb.wait(Sc, "t")
                        kb.op(Sc, "copy",
                              out=a32.ap(dst, seg * 2048 + c * 512,
                                         seg * 2048 + (c + 1) * 512),
                              in_=PB[c % 2][:, 0:512], inc=("s", 1))
                kb.wait(T, "s")
            for tt in range(p4_tiles):
                r0 = tt * 256
                kb.wait(Sy, "t")
                kb.wait(Sy, "s")
                kb.wait(Sy, "v")
                for hc in range(16):
                    kb.dma(Sy, a32.ap("aoc", hc * 256, (hc + 1) * 256),
                           aosp_d[hc * 128:(hc + 1) * 128, r0:r0 + 256], "dC")
                for tc in range(2):
                    kb.dma(Sy, a32.ap("ln2", tc * 2048, (tc + 1) * 2048),
                           xown_d[r0 + tc * 128:r0 + (tc + 1) * 128, :], "dC")
                wmm = []
                for hc in range(16):
                    par = hc % 2
                    if hc >= 2:
                        kb.wait(Sy, "t", wmm[hc - 2])
                    kb.dma(Sy, a32.ap("woc", par * 2048, (par + 1) * 2048),
                           woT_d[hc * 128:(hc + 1) * 128, :], f"dW{par}")
                    kb.wait(T, f"dW{par}")
                    if hc == 0:
                        kb.wait(T, "s")
                        kb.wait(T, "v")
                    last = None
                    for tc in range(2):
                        for dc in range(4):
                            last = kb.op(T, "matmul", PB[tc * 4 + dc][:, 0:512],
                                         a32.ap("aoc", hc * 256 + tc * 128,
                                                hc * 256 + (tc + 1) * 128),
                                         a32.ap("woc", par * 2048 + dc * 512,
                                                par * 2048 + (dc + 1) * 512),
                                         start=(hc == 0), stop=(hc == 15),
                                         inc=("t", 1))
                    wmm.append(last)
                kb.wait(V, "t")
                kb.wait(V, "dC")
                for tc in range(2):
                    for dc in range(4):
                        kb.op(V, "tensor_add",
                              out=a32.ap("resid", tc * 2048 + dc * 512,
                                         tc * 2048 + (dc + 1) * 512),
                              in0=PB[tc * 4 + dc][:, 0:512],
                              in1=a32.ap("ln2", tc * 2048 + dc * 512,
                                         tc * 2048 + (dc + 1) * 512))
                kb.op(V, "engine_nop", inc=("v", 1))
                if tt == 0 and "resid0" in dbg:
                    dump("resid0", a32.ap("resid"), after=("v",))
                    kb.wait(Sc, "dD")
                    kb.wait(T, "dD")
                kb.wait(Sc, "v")
                rap = lambda tc: a32.ap("resid", tc * 2048, (tc + 1) * 2048)
                lap = lambda tc: a32.ap("ln2", tc * 2048, (tc + 1) * 2048)
                emit_ln_super([rap(0), rap(1)], [lap(0), lap(1)], D, 1e-5,
                              a32.ap("rscr", 0, 2048))
                if tt == 0 and "ln20" in dbg:
                    dump("ln20", a32.ap("ln2"), after=("s",))
                    kb.wait(Sc, "dD")
                    kb.wait(V, "dD")
                    kb.wait(T, "dD")
                for tc in range(2):
                    kb.op(Sc, "activation", out=a32.ap("rscr", 0, 2048), in_=lap(tc),
                          func=AF.Square, accum_out=LC(16), inc=("s", 1))
                    kb.wait(V, "s")
                    kb.op(V, "tensor_single_scalar", out=LC(17), in_=LC(16),
                          scalar=1.0 / D, op=ALU.mult, inc=("v", 1))
                    kb.wait(Sc, "v")
                    kb.op(Sc, "activation", out=LC(18), in_=LC(17), func=AF.Sqrt,
                          bias=cst(1e-8), inc=("s", 1))
                    kb.wait(V, "s")
                    kb.op(V, "reciprocal", out=LC(19), in_=LC(18), inc=("v", 1))
                    for j, (gb, kT) in enumerate((("g1b", "kT1"), ("g2b", "kT2"))):
                        amaxc = emit_quant_cols(lap(tc), D, a32.ap(gb), LC(19),
                                                a16.ap("kbf"), f"q{j}",
                                                dbg_tag=(f"t{tc}j{j}" if tt == 0 else None))
                        kb.wait(V, "s")
                        kb.op(V, "scalar_tensor_tensor", out=LC(26 + 2 * tc + j),
                              in0=amaxc, scalar=1.0 / 127.0,
                              in1=a32.ap("acol", j + 1, j + 2),
                              op0=ALU.mult, op1=ALU.mult, inc=("v", 1))
                        kb.wait(T, "s")
                        kb.wait(T, "v")
                        for fc in range(16):
                            bank = PB[fc % 2]
                            kb.wait(T, "s")
                            kb.op(T, "transpose", bank[0:128, 0:64].bitcast(BF16),
                                  a16.ap("kbf", fc * 128, (fc + 1) * 128), IDB,
                                  inc=("t", 1))
                            kb.wait(Sc, "t")
                            kb.op(Sc, "copy",
                                  out=a16.ap(kT, fc * 256 + tc * 128,
                                             fc * 256 + (tc + 1) * 128),
                                  in_=bank[0:128, 0:64].bitcast(BF16), inc=("s", 1))
                if tt == 0 and "kt10" in dbg:
                    dump("kt10", a16.ap("kT1"), after=("s",))
                    kb.wait(Sc, "dD")
                    kb.wait(T, "dD")
                CPL = 6
                loads = [(0, 6), (6, 12), (12, 16)]
                kb.wait(Sy, "t")
                epi = []
                for fs in range(16):
                    fpar = fs % 2
                    if fs >= 2:
                        kb.wait(T, "s", epi[fs - 2][0])
                        kb.wait(T, "v", epi[fs - 2][1])
                    else:
                        kb.wait(T, "s")
                        kb.wait(T, "v")
                    for wj, (tch, kTn) in enumerate(((t1c_d, "kT1"), (t2c_d, "kT2"))):
                        for li, (f0, f1) in enumerate(loads):
                            par = (len(loads) * wj + li) % 2
                            g = len(EMM)
                            if g >= 2:
                                kb.wait(Sy, "t", EMM[g - 2])
                            for fc in range(f0, f1):
                                kb.dma(Sy, tstr(par * 3072 + (fc - f0) * 512,
                                                par * 3072 + (fc - f0 + 1) * 512),
                                       tch[fc, :, fs * 512:(fs + 1) * 512], f"dE{par}")
                            kb.wait(T, f"dE{par}")
                            last = None
                            for fc in range(f0, f1):
                                for tc in range(2):
                                    last = kb.op(T, "matmul",
                                          PB[4 * fpar + 2 * wj + tc][:, 0:512],
                                          a16.ap(kTn, fc * 256 + tc * 128,
                                                 fc * 256 + (tc + 1) * 128),
                                          tstr(par * 3072 + (fc - f0) * 512,
                                                 par * 3072 + (fc - f0 + 1) * 512),
                                          start=(fc == 0), stop=(fc == 15),
                                          inc=("t", 1))
                            EMM.append(last)
                    kb.wait(Sc, "t")
                    kb.wait(Sc, "v")
                    kb.wait(V, "t")
                    kb.wait(V, "g")
                    if tt == 0 and fs == 0 and "y1p" in dbg:
                        kb.op(Sc, "copy", out=a32.ap("rscr", 4096, 4608),
                              in_=PB[0][:, 0:512])
                        kb.op(Sc, "copy", out=a32.ap("rscr", 4608, 5120),
                              in_=PB[2][:, 0:512], inc=("s", 1))
                        dump("y1p", a32.ap("rscr", 4096, 4608), after=("s",))
                        dump("y2p", a32.ap("rscr", 4608, 5120), after=("s",))
                        dump("mcols", a32.ap("lncol", 26, 33), after=("s",))
                        kb.wait(Sc, "dD")
                        kb.wait(V, "dD")
                        kb.wait(T, "dD")
                    sN = vN = None
                    for tc in range(2):
                        sN = kb.op(Sc, "activation",
                              out=a32.ap("rscr", tc * 512, tc * 512 + 512),
                              in_=PB[4 * fpar + tc][:, 0:512], func=AF.Silu,
                              scale=LC(26 + 2 * tc + 0), inc=("s", 1))
                        kb.wait(V, "s")
                        vN = kb.op(V, "scalar_tensor_tensor",
                              out=a32.ap("rscr", 1024 + tc * 512, 1024 + (tc + 1) * 512),
                              in0=PB[4 * fpar + 2 + tc][:, 0:512], scalar=LC(26 + 2 * tc + 1),
                              in1=a32.ap("rscr", tc * 512, tc * 512 + 512),
                              op0=ALU.mult, op1=ALU.mult, inc=("v", 1))
                        kb.wait(G, "v")
                        kb.dma(G, hsp_d[tc, :, fs * 512:(fs + 1) * 512],
                               a32.ap("rscr", 1024 + tc * 512, 1024 + (tc + 1) * 512), "g")
                    epi.append((sN, vN))
                if tt == 0 and "h0" in dbg:
                    kb.wait(Sy, "g")
                    kb.wait(Sy, "v")
                    kb.wait(Sy, "s")
                    kb.dma(Sy, a32.ap("rscr", 0, 2048), hsp_d[0, :, 0:2048], "dC")
                    kb.wait(Sy, "dC")
                    kb.dma(Sy, dbg["h0"], a32.ap("rscr", 0, 2048), "dD")
                    kb.wait(Sy, "dD")
                    kb.wait(Sc, "dD")
                    kb.wait(V, "dD")
                for tc in range(2):
                    kb.wait(Sy, "t")
                    kb.wait(Sy, "g")
                    kb.wait(Sy, "v")
                    kb.wait(Sy, "s")
                    sqev = []
                    for sl in range(4):
                        par = sl % 2
                        if sl >= 2:
                            kb.wait(Sy, "s", sqev[sl - 2])
                        kb.dma(Sy, a32.ap("rscr", par * 2048, (par + 1) * 2048),
                               hsp_d[tc, :, sl * 2048:(sl + 1) * 2048], f"dH{par}")
                        kb.wait(Sc, f"dH{par}")
                        sqev.append(kb.op(Sc, "activation", out=a32.ap("rscr", 4096, 6144),
                              in_=a32.ap("rscr", par * 2048, (par + 1) * 2048),
                              func=AF.Square,
                              accum_out=LC(16 + sl), inc=("s", 1)))
                    kb.wait(V, "s")
                    kb.op(V, "tensor_add", out=LC(16), in0=LC(16), in1=LC(17))
                    kb.op(V, "tensor_add", out=LC(18), in0=LC(18), in1=LC(19), inc=("v", 1))
                    kb.wait(Sc, "v")
                    kb.op(Sc, "copy", out=LC(20), in_=LC(16))
                    kb.op(Sc, "copy", out=LC(21), in_=LC(18), inc=("s", 1))
                    kb.wait(V, "s")
                    kb.op(V, "tensor_add", out=LC(22), in0=LC(20), in1=LC(21), inc=("v", 1))
                    kb.wait(Sc, "v")
                    kb.op(Sc, "activation", out=LC(23), in_=LC(22), func=AF.Sqrt,
                          scale=1.0 / DH, bias=cst(1e-8), inc=("s", 1))
                    kb.wait(V, "s")
                    kb.op(V, "reciprocal", out=LC(24), in_=LC(23), inc=("v", 1))
                    kb.wait(Sc, "v")
                    kb.wait(Sy, "v")
                    amev = []
                    for sl in range(4):
                        par = sl % 2
                        if sl >= 2:
                            kb.wait(Sy, "s", amev[sl - 2])
                        kb.dma(Sy, a32.ap("rscr", par * 2048, (par + 1) * 2048),
                               hsp_d[tc, :, sl * 2048:(sl + 1) * 2048], f"dH{par}")
                        kb.wait(Sc, f"dH{par}")
                        kb.wait(Sc, "v")
                        amev.append(kb.op(Sc, "activation", out=a32.ap("rscr", 4096, 6144),
                              in_=a32.ap("rscr", par * 2048, (par + 1) * 2048),
                              func=AF.Identity,
                              scale=LC(24), inc=("s", 1)))
                        kb.wait(V, "s")
                        kb.op(V, "tensor_mul", out=a32.ap("rscr", 4096, 6144),
                              in0=a32.ap("rscr", 4096, 6144),
                              in1=a32.ap("g3b", sl * 2048, (sl + 1) * 2048))
                        kb.op(V, "reduce_max", out=LC(16 + sl),
                              in_=a32.ap("rscr", 4096, 6144), axis=AX.X,
                              apply_absolute_value=True, inc=("v", 1))
                    kb.wait(Sc, "v")
                    kb.op(Sc, "copy", out=LC(20), in_=LC(16))
                    kb.op(Sc, "copy", out=LC(21), in_=LC(18), inc=("s", 1))
                    kb.wait(V, "s")
                    kb.op(V, "tensor_max", out=LC(20), in0=LC(20), in1=LC(17))
                    kb.op(V, "tensor_max", out=LC(21), in0=LC(21), in1=LC(19), inc=("v", 1))
                    kb.wait(Sc, "v")
                    kb.op(Sc, "copy", out=LC(22), in_=LC(20), inc=("s", 1))
                    kb.wait(V, "s")
                    kb.op(V, "tensor_max", out=LC(22), in0=LC(22), in1=LC(21),
                          inc=("v", 1))
                    kb.wait(Sc, "v")
                    kb.op(Sc, "copy", out=LC(36), in_=LC(22), inc=("s", 1))
                    kb.wait(V, "s")
                    kb.op(V, "tensor_single_scalar", out=LC(23), in_=LC(36),
                          scalar=1e-4, op=ALU.max, inc=("v", 1))
                    kb.wait(Sc, "v")
                    kb.op(Sc, "copy", out=LC(35), in_=LC(23), inc=("s", 1))
                    kb.wait(V, "s")
                    kb.op(V, "reciprocal", out=LC(25), in_=LC(35), inc=("v", 1))
                    kb.wait(Sc, "v")
                    kb.op(Sc, "copy", out=LC(34), in_=LC(25), inc=("s", 1))
                    kb.wait(V, "s")
                    kb.op(V, "tensor_single_scalar", out=LC(31), in_=LC(34),
                          scalar=127.0, op=ALU.mult)
                    kb.op(V, "scalar_tensor_tensor", out=LC(32), in0=LC(23),
                          scalar=1.0 / 127.0, in1=a32.ap("acol", 3, 4),
                          op0=ALU.mult, op1=ALU.mult, inc=("v", 1))
                    kb.wait(Sc, "v")
                    for sl in range(4):
                        kb.wait(Sy, "v")
                        kb.wait(Sy, "gd")
                        kb.dma(Sy, a32.ap("rscr", 0, 2048),
                               hsp_d[tc, :, sl * 2048:(sl + 1) * 2048], "dC")
                        kb.wait(Sc, "dC")
                        kb.wait(Sc, "v")
                        kb.op(Sc, "activation", out=a32.ap("rscr", 0, 2048),
                              in_=a32.ap("rscr", 0, 2048), func=AF.Identity,
                              scale=LC(24), inc=("s", 1))
                        kb.wait(V, "s")
                        kb.op(V, "tensor_mul", out=a32.ap("rscr", 0, 2048),
                              in0=a32.ap("rscr", 0, 2048),
                              in1=a32.ap("g3b", sl * 2048, (sl + 1) * 2048),
                              inc=("v", 1))
                        kb.wait(Sc, "v")
                        kb.op(Sc, "activation", out=a32.ap("rscr", 0, 2048),
                              in_=a32.ap("rscr", 0, 2048), func=AF.Identity,
                              scale=LC(31), inc=("s", 1))
                        kb.wait(V, "s")
                        kb.op(V, "engine_nop", inc=("v", 1))
                        if tt == 0 and tc == 0 and sl == 0 and "y3q0" in dbg:
                            dump("y3q0", a32.ap("rscr", 0, 2048), after=("v", "s"))
                            kb.wait(V, "dD")
                            kb.wait(Sc, "dD")
                            kb.wait(G, "dD")
                            kb.wait(T, "dD")
                        ii3 = a32.ap("rscr", 2048, 4096).bitcast(I32)
                        emit_round(a32.ap("rscr", 0, 2048), a32.ap("rscr", 4096, 6144),
                                   ii3, a32.ap("rscr", 2048, 4096),
                                   a32.ap("aoc", 0, 2048), vsem="v", gsem="gd")
                        kb.op(V, "engine_nop", inc=("v", 1))
                        kb.wait(Sc, "v")
                        kb.op(Sc, "copy", out=a16.ap("kbf"),
                              in_=a32.ap("rscr", 4096, 6144), inc=("s", 1))
                        kb.wait(T, "s")
                        kb.wait(T, "v")
                        for fc in range(16):
                            bank = PB[fc % 2]
                            kb.wait(T, "s")
                            kb.op(T, "transpose", bank[0:128, 0:64].bitcast(BF16),
                                  a16.ap("kbf", fc * 128, (fc + 1) * 128), IDB,
                                  inc=("t", 1))
                            kb.wait(Sc, "t")
                            kb.op(Sc, "copy",
                                  out=a16.ap("k3Ts", fc * 128, (fc + 1) * 128),
                                  in_=bank[0:128, 0:64].bitcast(BF16), inc=("s", 1))
                        kb.wait(T, "s")
                        for dc in range(4):
                            bank = PB[4 + dc % 2]
                            if dc >= 2:
                                kb.wait(T, "v")
                            for fc in range(16):
                                par = fc % 2
                                g = len(Y3MM)
                                if g >= 2:
                                    kb.wait(Sy, "t", Y3MM[g - 2])
                                kb.dma(Sy, tstr(par * 1536, par * 1536 + 512),
                                       t3c_d[sl * 16 + fc, :, dc * 512:(dc + 1) * 512],
                                       f"dY{par}")
                                kb.wait(T, f"dY{par}")
                                Y3MM.append(kb.op(T, "matmul", bank[:, 0:512],
                                      a16.ap("k3Ts", fc * 128, (fc + 1) * 128),
                                      tstr(par * 1536, par * 1536 + 512),
                                      start=(fc == 0), stop=(fc == 15), inc=("t", 1)))
                            kb.wait(V, "t")
                            if sl == 0:
                                kb.op(V, "tensor_copy",
                                      out=a32.ap("outacc", tc * 2048 + dc * 512,
                                                 tc * 2048 + (dc + 1) * 512),
                                      in_=bank[:, 0:512], inc=("v", 1))
                            else:
                                kb.op(V, "tensor_add",
                                      out=a32.ap("outacc", tc * 2048 + dc * 512,
                                                 tc * 2048 + (dc + 1) * 512),
                                      in0=a32.ap("outacc", tc * 2048 + dc * 512,
                                                 tc * 2048 + (dc + 1) * 512),
                                      in1=bank[:, 0:512], inc=("v", 1))
                    if tt == 0 and tc == 0 and "oacc0" in dbg:
                        dump("oacc0", a32.ap("outacc", 0, 2048), after=("v", "s"))
                        dump("qcols", a32.ap("lncol", 16, 35), after=("v", "s"))
                        dump("k3s", a16.ap("k3Ts"), after=("s",))
                        kb.wait(V, "dD")
                        kb.wait(Sc, "dD")
                        kb.wait(T, "dD")
                    kb.wait(V, "s")
                    kb.op(V, "scalar_tensor_tensor", out=a32.ap("outacc", tc * 2048,
                                                                (tc + 1) * 2048),
                          in0=a32.ap("outacc", tc * 2048, (tc + 1) * 2048),
                          scalar=LC(32), in1=rap(tc), op0=ALU.mult, op1=ALU.add,
                          inc=("v", 1))
                    kb.wait(Sy, "v")
                    kb.dma(Sy, out_d[r0 + tc * 128:r0 + (tc + 1) * 128, :],
                           a32.ap("outacc", tc * 2048, (tc + 1) * 2048), "dO")
            kb.wait(Sy, "dO")

        if "P4" in phases:
            emit_p4()
            if "ao0" in dbg:
                a32.alloc("probe3", 2048)
                for nm, h in (("ao0", 0), ("ao9", 9)):
                    kb.wait(Sy, "dB")
                    kb.dma(Sy, a32.ap("probe3"), aosp_d[h * VD:(h + 1) * VD, :], "dB")
                    kb.wait(Sy, "dB")
                    kb.dma(Sy, dbg[nm], a32.ap("probe3"), "dD")
                    kb.wait(Sy, "dD")
                a32.free("probe3")
            if "qh0" in dbg:
                a32.alloc("probe2", 2048)
                for nm, h in (("qh0", 0), ("qh7", 7)):
                    kb.wait(Sy, "dA")
                    kb.dma(Sy, a32.ap("probe2"), qsp_d[h * QHD:h * QHD + 128, :], "dA")
                    kb.wait(Sy, "dA")
                    kb.dma(Sy, dbg[nm], a32.ap("probe2"), "dD")
                    kb.wait(Sy, "dD")
                    kb.dma(Sy, a32.ap("probe2", 0, 2048, 0, 64),
                           qsp_d[h * QHD + 128:h * QHD + 192, :], "dA")
                    kb.wait(Sy, "dA")
                    kb.dma(Sy, dbg[nm + "r"], a32.ap("probe2", 0, 2048, 0, 64), "dD")
                    kb.wait(Sy, "dD")
                a32.free("probe2")

        kb.finalize()
    return nc, kb



def _own_rows(core):
    h2 = core % 2
    return np.concatenate([np.arange(512 * j + 256 * h2, 512 * j + 256 * h2 + 256)
                           for j in range(8)])


def kernel(L0_attn_hidden_states, L0_attn_q_proj_w, L0_attn_kv_a_proj_w,
           L0_attn_kv_a_norm_w, L0_attn_kv_b_proj_w, L0_attn_o_proj_w,
           L0_ffn_w1, L0_ffn_g1, L0_ffn_w2, L0_ffn_g2, L0_ffn_w3, L0_ffn_g3):
    x = np.asarray(L0_attn_hidden_states, np.float32)
    wqT = np.ascontiguousarray(np.asarray(L0_attn_q_proj_w, np.float32).T)
    waT = np.ascontiguousarray(np.asarray(L0_attn_kv_a_proj_w, np.float32).T)
    wkvbT = np.ascontiguousarray(np.asarray(L0_attn_kv_b_proj_w, np.float32).T)
    woT = np.ascontiguousarray(np.asarray(L0_attn_o_proj_w, np.float32).T)
    w1T = np.ascontiguousarray(np.asarray(L0_ffn_w1, np.float32).T)
    w2T = np.ascontiguousarray(np.asarray(L0_ffn_w2, np.float32).T)
    w3T = np.ascontiguousarray(np.asarray(L0_ffn_w3, np.float32).T)
    kvnw = np.ascontiguousarray(np.asarray(L0_attn_kv_a_norm_w, np.float32).reshape(1, RANK))
    g1 = np.ascontiguousarray(np.asarray(L0_ffn_g1, np.float32).reshape(1, D))
    g2 = np.ascontiguousarray(np.asarray(L0_ffn_g2, np.float32).reshape(1, D))
    g3 = np.ascontiguousarray(np.asarray(L0_ffn_g3, np.float32).reshape(1, DH))
    maps = []
    for c in range(8):
        b = c // 2
        rows = _own_rows(c)
        maps.append({
            "xfull": np.ascontiguousarray(x[b]),
            "xown": np.ascontiguousarray(x[b][rows]),
            "qpos": np.ascontiguousarray(rows.reshape(16, 128).T.astype(np.float32)),
            "qposf": np.ascontiguousarray(
                np.broadcast_to(rows[None, :], (64, S // 2)).astype(np.float32)),
            "wqT": wqT, "waT": waT, "wkvbT": wkvbT, "woT": woT,
            "w1T": w1T, "w2T": w2T, "w3T": w3T,
            "kvnw": kvnw, "g1": g1, "g2": g2, "g3": g3,
        })
    nc, _kb = build_program()
    res = run_bass_kernel_spmd(nc, maps, core_ids=list(range(8)))
    out = np.zeros((B, S, D), np.float32)
    for c in range(8):
        out[c // 2][_own_rows(c)] = res.results[c]["out"]
    return out


# version 11
# speedup vs baseline: 1.0156x; 1.0156x over previous
import math
import contextlib
import numpy as np
import concourse.bass as bass
import concourse.mybir as mybir
from concourse.bass_utils import run_bass_kernel_spmd

F32 = mybir.dt.float32
BF16 = mybir.dt.bfloat16
I32 = mybir.dt.int32
AF = mybir.ActivationFunctionType
ALU = mybir.AluOpType
AX = mybir.AxisListType

B, S, D = 4, 4096, 2048
H, NOPE, ROPE, VD, RANK = 16, 128, 64, 128, 512
QHD = NOPE + ROPE
DH = 8192
SCALE = float(np.float32(1.0) / np.sqrt(np.float32(QHD)))
NEG = -1.0e20


def _ds(x):
    hi = np.float32(x)
    lo = np.float32(x - np.float64(hi))
    return float(hi), float(lo)


INV2PI_HI, INV2PI_LO = _ds(1.0 / (2 * math.pi))
TWOPI_HI, TWOPI_LO = _ds(2 * math.pi)
PIHALF = float(np.float32(math.pi / 2))

ENGINES = ("sync", "tensor", "vector", "scalar", "gpsimd")
V, Sc, T, G, Sy = "vector", "scalar", "tensor", "gpsimd", "sync"


class SemCtr:
    def __init__(self, name):
        self.name = name
        self.total = 0


class KB:

    def __init__(self, nc):
        self.nc = nc
        self.ops = {e: [] for e in ENGINES}
        self.sems = {}
        self.sem_handles = {}
        self.ninstr = 0

    def sem(self, name):
        if name not in self.sems:
            self.sems[name] = SemCtr(name)
        return self.sems[name]

    def wait(self, eng, sem_name, value=None):
        ctr = self.sem(sem_name)
        v = ctr.total if value is None else value
        assert v <= ctr.total, (sem_name, v, ctr.total)
        if v <= 0:
            return
        self.ops[eng].append(lambda e, _s=sem_name, _v=v: e.wait_ge(self.sem_handles[_s], _v))

    def op(self, eng, method, *args, inc=None, **kwargs):
        self.ninstr += 1
        if inc is None:
            self.ops[eng].append(lambda e: getattr(e, method)(*args, **kwargs))
            return None
        sem_name, n = inc
        ctr = self.sem(sem_name)
        ctr.total += n
        def g(e, _m=method, _a=args, _k=kwargs, _s=sem_name, _n=n):
            getattr(e, _m)(*_a, **_k).then_inc(self.sem_handles[_s], _n)
        self.ops[eng].append(g)
        return ctr.total

    def dma(self, eng, out_ap, in_ap, sem_name):
        return self.op(eng, "dma_start", out=out_ap, in_=in_ap, inc=(sem_name, 16))

    def barrier(self):
        for eng in ENGINES:
            for sem_name, ctr in self.sems.items():
                if ctr.total > 0:
                    self.wait(eng, sem_name)

    def finalize(self):
        nc = self.nc
        with contextlib.ExitStack() as stack:
            for name in self.sems:
                self.sem_handles[name] = stack.enter_context(nc.semaphore(name))
            with nc.Block() as block:
                for eng_name in ENGINES:
                    fns = self.ops[eng_name]
                    if fns:
                        getattr(block, eng_name)(lambda e, _f=fns: [f(e) for f in _f])


class Arena:
    def __init__(self, tensor, words, name):
        self.t = tensor
        self.name = name
        self.regions = {}
        self.free_list = [(0, words)]
        self.peak = 0

    def alloc(self, name, size):
        assert name not in self.regions, name
        for i, (off, sz) in enumerate(self.free_list):
            if sz >= size:
                self.regions[name] = (off, size)
                if sz == size:
                    self.free_list.pop(i)
                else:
                    self.free_list[i] = (off + size, sz - size)
                self.peak = max(self.peak, max(o + s for (o, s) in
                                               [(v[0], v[1]) for v in self.regions.values()]))
                return off
        raise MemoryError(f"{self.name}: alloc {name}[{size}] failed; free={self.free_list}")

    def free(self, *names):
        for name in names:
            off, size = self.regions.pop(name)
            self.free_list.append((off, size))
        self.free_list.sort()
        merged = []
        for off, sz in self.free_list:
            if merged and merged[-1][0] + merged[-1][1] == off:
                merged[-1] = (merged[-1][0], merged[-1][1] + sz)
            else:
                merged.append((off, sz))
        self.free_list = merged

    def ap(self, name, lo=0, hi=None, p0=0, p1=128):
        off, size = self.regions[name]
        hi = size if hi is None else hi
        assert 0 <= lo <= hi <= size, (name, lo, hi, size)
        return self.t[p0:p1, off + lo: off + hi]


def build_program(phases=("P0a", "P0b", "P1", "P2", "P3", "P4"), debug_dumps=(),
                  p3_heads=H, p4_tiles=8):
    nc = bass.Bass("TRN2", target_bir_lowering=False)

    xfull_d = nc.dram_tensor("xfull", [S, D], F32, kind="ExternalInput").ap()
    xown_d = nc.dram_tensor("xown", [S // 2, D], F32, kind="ExternalInput").ap()
    qpos_d = nc.dram_tensor("qpos", [128, 16], F32, kind="ExternalInput").ap()
    qposf_d = nc.dram_tensor("qposf", [64, S // 2], F32, kind="ExternalInput").ap()
    wqT_d = nc.dram_tensor("wqT", [D, H * QHD], F32, kind="ExternalInput").ap()
    waT_d = nc.dram_tensor("waT", [D, RANK + ROPE], F32, kind="ExternalInput").ap()
    wkvbT_d = nc.dram_tensor("wkvbT", [RANK, H * (NOPE + VD)], F32, kind="ExternalInput").ap()
    woT_d = nc.dram_tensor("woT", [H * VD, D], F32, kind="ExternalInput").ap()
    w1T_d = nc.dram_tensor("w1T", [D, DH], F32, kind="ExternalInput").ap()
    w2T_d = nc.dram_tensor("w2T", [D, DH], F32, kind="ExternalInput").ap()
    w3T_d = nc.dram_tensor("w3T", [DH, D], F32, kind="ExternalInput").ap()
    kvnw_d = nc.dram_tensor("kvnw", [1, RANK], F32, kind="ExternalInput").ap()
    g1_d = nc.dram_tensor("g1", [1, D], F32, kind="ExternalInput").ap()
    g2_d = nc.dram_tensor("g2", [1, D], F32, kind="ExternalInput").ap()
    g3_d = nc.dram_tensor("g3", [1, DH], F32, kind="ExternalInput").ap()
    out_d = nc.dram_tensor("out", [S // 2, D], F32, kind="ExternalOutput").ap()

    qsp_d = nc.dram_tensor("qsp", [H * QHD, S // 2], F32, kind="Internal").ap()
    aosp_d = nc.dram_tensor("aosp", [H * VD, S // 2], F32, kind="Internal").ap()
    hsp_d = nc.dram_tensor("hsp", [2, 128, DH], F32, kind="Internal").ap()
    t1c_d = nc.dram_tensor("t1c", [16, 128, DH], BF16, kind="Internal").ap()
    t2c_d = nc.dram_tensor("t2c", [16, 128, DH], BF16, kind="Internal").ap()
    t3c_d = nc.dram_tensor("t3c", [64, 128, D], BF16, kind="Internal").ap()
    wcache_d = {1: t1c_d, 2: t2c_d, 3: t3c_d}

    dbg = {}
    for spec in debug_dumps:
        name, shape = spec[0], spec[1]
        dt_ = spec[2] if len(spec) > 2 else F32
        dbg[name] = nc.dram_tensor("dbg_" + name, list(shape), dt_, kind="ExternalOutput").ap()

    kb = KB(nc)
    with contextlib.ExitStack() as stack:
        A32W, A16W = 45700, 14500
        a32_t = stack.enter_context(nc.sbuf_tensor("A32", [128, A32W], F32))
        a16_t = stack.enter_context(nc.sbuf_tensor("A16", [128, A16W], BF16))
        a32 = Arena(a32_t, A32W, "a32")
        a16 = Arena(a16_t, A16W, "a16")
        PB = [stack.enter_context(nc.psum_tensor(f"PB{i}", [128, 512], F32)) for i in range(8)]

        def dump(name, ap, after=()):
            if name not in dbg:
                return False
            for sem_name in after:
                kb.wait(Sy, sem_name)
            kb.dma(Sy, dbg[name], ap, "dD")
            return True

        a32.alloc("identf", 128)
        a32.alloc("onescol", 1)
        a32.alloc("onesrow", 128)
        a32.alloc("qpos", 16)
        a32.alloc("sc", 256)
        a32.alloc("invfb", 64)
        a32.alloc("onesm", 128)
        a32.alloc("cst", 8)
        CSTS = {}
        def cst(v):
            assert v in CSTS, v
            return a32.ap("cst", CSTS[v], CSTS[v] + 1)
        def reg_cst(v):
            if v not in CSTS:
                CSTS[v] = len(CSTS)
                kb.op(G, "memset", a32.ap("cst", CSTS[v], CSTS[v] + 1), float(v))
            return cst(v)
        a32.alloc("pcol", 16)
        a16.alloc("identb", 128)
        a32.alloc("acol", 4)
        a32.alloc("thcol", 2)
        a32.alloc("lncol", 40)
        a32.alloc("cosq", 2048)
        a32.alloc("sinq", 2048)
        a32.alloc("ckvT", 4 * 4096)
        a32.alloc("kp", 4096)
        a32.alloc("swap64", 64)
        IDF = a32.ap("identf")
        IDB = a16.ap("identb")
        ONESC = a32.ap("onescol")
        ONESR = a32.ap("onesrow", p0=0, p1=1)

        a32.alloc("cosk", 2048)
        a32.alloc("sink", 2048)
        a32.alloc("tabscr", 10240)

        def emit_round(y_ap, nf_ap, int_ap, b_ap, c_ap, vsem="v", gsem="gd"):
            kb.wait(G, vsem)
            kb.dma(G, int_ap, y_ap, gsem)
            kb.wait(G, gsem)
            kb.dma(G, nf_ap, int_ap, gsem)
            kb.wait(V, gsem)
            kb.op(V, "tensor_sub", out=b_ap, in0=y_ap, in1=nf_ap)
            kb.op(V, "tensor_single_scalar", out=c_ap, in_=b_ap, scalar=0.5, op=ALU.is_gt)
            kb.op(V, "tensor_add", out=nf_ap, in0=nf_ap, in1=c_ap)
            kb.op(V, "tensor_single_scalar", out=c_ap, in_=b_ap, scalar=-0.5, op=ALU.is_lt)
            kb.op(V, "tensor_sub", out=nf_ap, in0=nf_ap, in1=c_ap)

        def emit_sincos(src_ap, p0, p1, w, scr_name, cos_ap, sin_ap, tag):
            t = lambda lo, hi: a32.ap(scr_name, lo * w, hi * w, p0, p1)
            ii = a32.ap(scr_name, 3 * w, 4 * w, p0, p1).bitcast(I32)
            kb.op(V, "tensor_single_scalar", out=t(0, 1), in_=src_ap, scalar=INV2PI_HI, op=ALU.mult)
            kb.op(V, "scalar_tensor_tensor", out=t(0, 1), in0=src_ap, scalar=INV2PI_LO,
                  in1=t(0, 1), op0=ALU.mult, op1=ALU.add, inc=("v", 1))
            kb.wait(G, "v")
            kb.dma(G, ii, t(0, 1), "gd")
            kb.wait(G, "gd")
            kb.dma(G, t(1, 2), ii, "gd")
            kb.wait(V, "gd")
            kb.op(V, "tensor_sub", out=t(2, 3), in0=t(0, 1), in1=t(1, 2))
            kb.op(V, "tensor_single_scalar", out=t(3, 4), in_=t(2, 3), scalar=0.5, op=ALU.is_gt)
            kb.op(V, "tensor_add", out=t(1, 2), in0=t(1, 2), in1=t(3, 4))
            kb.op(V, "tensor_single_scalar", out=t(3, 4), in_=t(2, 3), scalar=-0.5, op=ALU.is_lt)
            kb.op(V, "tensor_sub", out=t(1, 2), in0=t(1, 2), in1=t(3, 4))
            kb.op(V, "scalar_tensor_tensor", out=t(0, 1), in0=t(1, 2), scalar=-TWOPI_HI,
                  in1=src_ap, op0=ALU.mult, op1=ALU.add)
            kb.op(V, "scalar_tensor_tensor", out=t(0, 1), in0=t(1, 2), scalar=-TWOPI_LO,
                  in1=t(0, 1), op0=ALU.mult, op1=ALU.add)
            kb.op(V, "scalar_tensor_tensor", out=t(1, 2), in0=t(0, 1), scalar=-1.0,
                  in1=t(0, 1), op0=ALU.mult, op1=ALU.max)
            kb.op(V, "tensor_scalar", out=t(1, 2), in0=t(1, 2), scalar1=-1.0, scalar2=PIHALF,
                  op0=ALU.mult, op1=ALU.add, inc=("v", 1))
            kb.wait(Sc, "v")
            kb.op(Sc, "activation", out=sin_ap, in_=t(0, 1), func=AF.Sin)
            kb.op(Sc, "activation", out=cos_ap, in_=t(1, 2), func=AF.Sin, inc=("s", 1))

        def emit_p0a():
            for v in (1e-8, 1e-5, 1e-6):
                reg_cst(v)
            kb.op(G, "memset", IDF, 1.0)
            kb.op(G, "affine_select", IDF, IDF, [[1, 128]], ALU.is_equal, 0.0,
                  base=0, channel_multiplier=-1)
            kb.op(G, "memset", ONESC, 1.0)
            kb.op(G, "memset", ONESR, 1.0)
            kb.op(G, "memset", a32.ap("onesm"), 1.0)
            kb.op(G, "memset", a32.ap("sc", 0, 1, 0, 64), 10000.0)
            kb.op(G, "iota", a32.ap("sc", 1, 2, 0, 32), [[1, 1]], base=0,
                  channel_multiplier=1, allow_small_or_imprecise_dtypes=True)
            kb.op(G, "iota", a32.ap("sc", 1, 2, 32, 64), [[1, 1]], base=0,
                  channel_multiplier=1, allow_small_or_imprecise_dtypes=True)
            kb.op(G, "iota", a32.ap("sc", 8, 8 + 64, 0, 1), [[0, 2], [1, 32]], base=0,
                  channel_multiplier=0, allow_small_or_imprecise_dtypes=True, inc=("gp", 1))
            kb.wait(Sc, "gp")
            kb.op(Sc, "activation", out=a32.ap("sc", 2, 3, 0, 64), in_=a32.ap("sc", 0, 1, 0, 64),
                  func=AF.Ln, inc=("s", 1))
            kb.wait(V, "s")
            kb.op(V, "tensor_single_scalar", out=a32.ap("sc", 3, 4, 0, 64),
                  in_=a32.ap("sc", 1, 2, 0, 64), scalar=a32.ap("sc", 2, 3, 0, 64), op=ALU.mult)
            kb.op(V, "tensor_single_scalar", out=a32.ap("sc", 8 + 64, 8 + 128, 0, 1),
                  in_=a32.ap("sc", 8, 8 + 64, 0, 1), scalar=a32.ap("sc", 2, 3, 0, 1),
                  op=ALU.mult, inc=("v", 1))
            kb.wait(Sc, "v")
            kb.op(Sc, "activation", out=a32.ap("sc", 4, 5, 0, 64), in_=a32.ap("sc", 3, 4, 0, 64),
                  func=AF.Exp, scale=-1.0 / 32.0)
            kb.op(Sc, "activation", out=a32.ap("sc", 8 + 128, 8 + 192, 0, 1),
                  in_=a32.ap("sc", 8 + 64, 8 + 128, 0, 1),
                  func=AF.Exp, scale=-1.0 / 32.0, inc=("s", 1))
            kb.wait(T, "s")
            kb.op(T, "matmul", PB[0][:, 0:64], ONESR, a32.ap("sc", 8 + 128, 8 + 192, 0, 1),
                  start=True, stop=True, inc=("t", 1))
            kb.wait(Sc, "t")
            kb.op(Sc, "copy", out=a32.ap("invfb"), in_=PB[0][:, 0:64], inc=("s", 1))
            kb.op(G, "iota", a32.ap("tabscr", 0, 32), [[128, 32]], base=0,
                  channel_multiplier=1, allow_small_or_imprecise_dtypes=True, inc=("gp", 1))
            kb.wait(V, "gp")
            kb.wait(V, "s")
            if dump("posb2", a32.ap("tabscr", 0, 32), after=("gp",)):
                kb.wait(V, "dD")
            for kt in range(32):
                kb.op(V, "tensor_single_scalar", out=a32.ap("cosk", kt * 64, kt * 64 + 64),
                      in_=a32.ap("invfb"), scalar=a32.ap("tabscr", kt, kt + 1),
                      op=ALU.mult)
            kb.op(V, "engine_nop", inc=("v", 1))
            if dump("freqs2", a32.ap("cosk"), after=("v",)):
                kb.wait(V, "dD")
                kb.wait(G, "dD")
            emit_sincos(a32.ap("cosk"), 0, 128, 2048, "tabscr", a32.ap("cosk"),
                        a32.ap("sink"), "k")
            kb.wait(Sy, "s")
            kb.dma(Sy, a32.ap("tabscr", 0, 2048, 0, 64), qposf_d, "dA")
            kb.wait(V, "dA")
            kb.op(V, "tensor_single_scalar", out=a32.ap("cosq", 0, 2048, 0, 64),
                  in_=a32.ap("tabscr", 0, 2048, 0, 64), scalar=a32.ap("sc", 4, 5, 0, 64),
                  op=ALU.mult, inc=("v", 1))
            emit_sincos(a32.ap("cosq", 0, 2048, 0, 64), 0, 64, 2048, "tabscr",
                        a32.ap("cosq", 0, 2048, 0, 64), a32.ap("sinq", 0, 2048, 0, 64), "q")
            kb.wait(V, "s")
            kb.op(V, "tensor_single_scalar", out=a32.ap("sinq", 0, 2048, 0, 32),
                  in_=a32.ap("sinq", 0, 2048, 0, 32), scalar=-1.0, op=ALU.mult,
                  inc=("v", 1))
            kb.op(Sc, "copy", out=IDB, in_=IDF, inc=("s", 1))
            kb.op(G, "memset", a32.ap("swap64", 0, 64, 0, 64), 0.0, inc=("gp", 1))
            kb.wait(Sc, "gp")
            kb.op(Sc, "copy", out=a32.ap("swap64", 32, 64, 0, 32),
                  in_=IDF[0:32, 0:32])
            kb.op(Sc, "copy", out=a32.ap("swap64", 0, 32, 32, 64),
                  in_=IDF[32:64, 32:64], inc=("s", 1))
            kb.dma(Sy, a32.ap("qpos"), qpos_d, "dA")
            dump("cosk", a32.ap("cosk"), after=("s",))
            dump("sink", a32.ap("sink"), after=("s",))
            dump("cosq", a32.ap("cosq", 0, 2048, 0, 64), after=("s", "v"))
            dump("sinq", a32.ap("sinq", 0, 2048, 0, 64), after=("s", "v"))
            dump("invfb", a32.ap("invfb"), after=("s",))

        if "P0a" in phases:
            emit_p0a()
            kb.barrier()
            a32.free("sc", "invfb")
            dump("coskA", a32.ap("cosk"), after=("s", "v", "gd"))
            if "coskA" in dbg:
                kb.wait(V, "dD")
                kb.wait(Sc, "dD")
                kb.wait(G, "dD")
                kb.wait(T, "dD")
            a32.free("tabscr")


        W_SPECS = {1: (w1T_d, 16, DH), 2: (w2T_d, 16, DH), 3: (w3T_d, 64, D)}

        def emit_p0b_weight(wi):
            wd, nchunks, cw = W_SPECS[wi]
            a32.alloc("wbuf", 2 * 4096)
            a32.alloc("accvec", 192)
            a16.alloc("tb1", 4096)
            a16.alloc("tb2", 4096)
            hw = min(cw, 4096)
            nh = cw // hw
            units = [(i, hc) for i in range(nchunks) for hc in range(nh)]
            wb = lambda par: a32.ap("wbuf", par * 4096, par * 4096 + hw)
            vC = []
            for u, (i, hc) in enumerate(units):
                if u >= 2:
                    kb.wait(Sy, "v", vC[u - 2])
                kb.dma(Sy, wb(u % 2), wd[i * 128:(i + 1) * 128, hc * hw:(hc + 1) * hw], "dA")
                kb.wait(V, "dA")
                vC.append(kb.op(V, "reduce_sum", out=a32.ap("accvec", u, u + 1),
                                in_=wb(u % 2), axis=AX.X, apply_absolute_value=True,
                                inc=("v", 1)))
            kb.wait(Sc, "v")
            kb.op(Sc, "activation", out=a32.ap("accvec", 96, 96 + len(units)),
                  in_=a32.ap("accvec", 0, len(units)), func=AF.Identity,
                  accum_out=a32.ap("pcol", 0, 1), inc=("s", 1))
            kb.wait(T, "s")
            kb.wait(T, "v")
            kb.op(T, "matmul", PB[6][:, 0:1], a32.ap("onesm"), a32.ap("pcol", 0, 1),
                  start=True, stop=True, inc=("t", 1))
            kb.wait(V, "t")
            PC = lambda c: a32.ap("pcol", c, c + 1)
            s_, seps, u_, vv, lnu, lnv, dd = (PC(1), PC(2), PC(3), PC(4), PC(5), PC(6), PC(7))
            ath = PC(9)
            a_ = a32.ap("acol", wi, wi + 1)
            kb.op(V, "tensor_single_scalar", out=s_, in_=PB[6][:, 0:1],
                  scalar=float(1.0 / (1 << 24)), op=ALU.mult, inc=("v", 1))
            kb.wait(Sc, "v")
            kb.op(Sc, "activation", out=seps, in_=s_, func=AF.Identity, bias=cst(1e-8))
            kb.op(Sc, "activation", out=u_, in_=s_, func=AF.Identity, bias=ONESC)
            kb.op(Sc, "activation", out=vv, in_=s_, func=AF.Identity, scale=-1.0,
                  bias=ONESC)
            kb.op(Sc, "activation", out=lnu, in_=u_, func=AF.Ln)
            kb.op(Sc, "activation", out=lnv, in_=vv, func=AF.Ln, inc=("s", 1))
            kb.wait(V, "s")
            kb.op(V, "tensor_sub", out=dd, in0=lnu, in1=lnv)
            kb.op(V, "tensor_tensor", out=PC(11), in0=seps, in1=ath, op=ALU.mult,
                  inc=("v", 1))
            kb.wait(Sc, "v")
            kb.op(Sc, "activation", out=a_, in_=dd, func=AF.Identity, scale=0.5)
            kb.op(Sc, "copy", out=a32.ap("thcol", 0, 1), in_=PC(11))
            kb.op(Sc, "activation", out=a32.ap("thcol", 1, 2), in_=PC(11),
                  func=AF.Identity, scale=-1.0, inc=("s", 1))
            kb.wait(V, "s")
            if dump(f"th{wi}", a32.ap("thcol", 0, 2), after=("s",)):
                kb.wait(V, "dD")
                kb.wait(Sc, "dD")
                kb.wait(T, "dD")
            cache = wcache_d[wi]
            vC2 = []
            gC2 = []
            for u, (i, hc) in enumerate(units):
                if u >= 2:
                    kb.wait(Sy, "v", vC2[u - 2])
                kb.dma(Sy, wb(u % 2), wd[i * 128:(i + 1) * 128, hc * hw:(hc + 1) * hw], "dA")
                kb.wait(V, "dA")
                if gC2:
                    kb.wait(V, "g", gC2[-1])
                wsl = wb(u % 2)
                kb.op(V, "tensor_scalar", out=a16.ap("tb1", 0, hw), in0=wsl,
                      scalar1=a32.ap("thcol", 0, 1), scalar2=None, op0=ALU.is_ge)
                kb.op(V, "tensor_scalar", out=a16.ap("tb2", 0, hw), in0=wsl,
                      scalar1=a32.ap("thcol", 1, 2), scalar2=-1.0,
                      op0=ALU.is_le, op1=ALU.mult)
                last = kb.op(V, "tensor_add", out=a16.ap("tb1", 0, hw),
                             in0=a16.ap("tb1", 0, hw), in1=a16.ap("tb2", 0, hw),
                             inc=("v", 1))
                kb.wait(G, "v")
                gC2.append(kb.dma(G, cache[i, :, hc * hw:(hc + 1) * hw],
                                  a16.ap("tb1", 0, hw), "g"))
                vC2.append(last)
            kb.wait(Sy, "g")
            a32.free("wbuf", "accvec")
            a16.free("tb1", "tb2")

        def emit_p0b():
            kb.op(G, "memset", a32.ap("pcol", 8, 9), 3.0, inc=("gp", 1))
            kb.wait(Sc, "gp")
            kb.op(Sc, "activation", out=a32.ap("pcol", 10, 11), in_=a32.ap("pcol", 8, 9),
                  func=AF.Ln, inc=("s", 1))
            kb.wait(V, "s")
            kb.op(V, "tensor_single_scalar", out=a32.ap("pcol", 9, 10),
                  in_=a32.ap("pcol", 10, 11), scalar=0.5, op=ALU.mult, inc=("v", 1))
            for wi in (1, 2, 3):
                emit_p0b_weight(wi)
            dump("svals", a32.ap("pcol", 0, 16, 0, 1), after=("v", "s"))
            dump("acols", a32.ap("acol"), after=("s",))
            if "t1probe" in dbg:
                a16.alloc("probe", 8192)
                kb.wait(Sy, "g")
                kb.dma(Sy, a16.ap("probe"), t1c_d[3], "dA")
                kb.wait(Sy, "dA")
                kb.dma(Sy, dbg["t1probe"], a16.ap("probe"), "dD")
                kb.dma(Sy, a16.ap("probe", 0, 2048), t3c_d[5], "dA")
                kb.wait(Sy, "dA")
                kb.dma(Sy, dbg["t3probe"], a16.ap("probe", 0, 2048), "dD")
                kb.wait(Sy, "dD")
                a16.free("probe")

        if "P0b" in phases:
            emit_p0b()
            kb.barrier()
            dump("coskB", a32.ap("cosk"), after=("s", "v", "g"))
            if "coskB" in dbg:
                kb.wait(V, "dD")
                kb.wait(Sc, "dD")
                kb.wait(G, "dD")
                kb.wait(T, "dD")
                kb.wait(Sy, "dD")


        def LC(c, n=1):
            return a32.ap("lncol", c, c + n)

        TRT = {"n": 0, "ev": []}

        def transpose_evac(src_aps, dst_aps, ident, pbanks=(4, 5)):
            kb.wait(T, "v")
            for i, (sap, dap) in enumerate(zip(src_aps, dst_aps)):
                bank = PB[pbanks[i % 2]]
                w = sap.shape[-1]
                if TRT["n"] >= 2:
                    kb.wait(T, "s", TRT["ev"][TRT["n"] - 2])
                kb.op(T, "transpose", bank[0:w, 0:128], sap, ident, inc=("t", 1))
                kb.wait(Sc, "t")
                TRT["ev"].append(kb.op(Sc, "copy", out=dap, in_=bank[0:w, 0:128],
                                       inc=("s", 1)))
                TRT["n"] += 1

        def emit_ln_super(x_aps, out_aps, w, eps, scr_ap, cb=0):
            n = len(x_aps)
            s1, s2, m, e2, nmsq, nme, var, sd, inv = (LC(cb + 4 * i, n) for i in range(9))
            for i, xa in enumerate(x_aps):
                kb.op(V, "reduce_sum", out=LC(cb + 0 * 4 + i), in_=xa, axis=AX.X,
                      inc=("v", 1) if i == n - 1 else None)
            kb.wait(Sc, "v")
            for i, xa in enumerate(x_aps):
                kb.op(Sc, "activation", out=scr_ap, in_=xa, func=AF.Square,
                      accum_out=LC(cb + 4 + i))
            kb.op(Sc, "activation", out=m, in_=s1, func=AF.Identity,
                  scale=1.0 / w, inc=("s", 1))
            kb.wait(V, "s")
            kb.op(V, "tensor_single_scalar", out=e2, in_=s2, scalar=1.0 / w, op=ALU.mult)
            kb.op(V, "scalar_tensor_tensor", out=nmsq, in0=m, scalar=-1.0, in1=m,
                  op0=ALU.mult, op1=ALU.mult, inc=("v", 1))
            kb.wait(Sc, "v")
            kb.op(Sc, "activation", out=nme, in_=nmsq, func=AF.Identity,
                  bias=cst(eps), inc=("s", 1))
            kb.wait(V, "s")
            kb.op(V, "tensor_add", out=var, in0=e2, in1=nme, inc=("v", 1))
            kb.wait(Sc, "v")
            kb.op(Sc, "activation", out=sd, in_=var, func=AF.Sqrt, inc=("s", 1))
            kb.wait(V, "s")
            kb.op(V, "reciprocal", out=inv, in_=sd)
            for i, xa in enumerate(x_aps):
                kb.op(V, "tensor_single_scalar", out=out_aps[i], in_=xa,
                      scalar=LC(cb + 8 + i), op=ALU.subtract,
                      inc=("v", 1) if i == n - 1 else None)
            kb.wait(Sc, "v")
            for i in range(n):
                kb.op(Sc, "activation", out=out_aps[i], in_=out_aps[i], func=AF.Identity,
                      scale=LC(cb + 32 + i), inc=("s", 1) if i == n - 1 else None)
            kb.wait(V, "s")


        TCT = {"kva": []}

        def emit_p1():
            a32.alloc("wbcast", 512)
            a32.alloc("xts", 4 * 2048)
            a32.alloc("lnscr", 2048)
            a32.alloc("lnxTt", 4 * 128)
            a32.alloc("waTc", 2 * 576)
            a32.alloc("ckscr", 512)
            a32.alloc("cktmp", 512)
            kb.dma(Sy, a32.ap("ckscr", 0, 512, 0, 1), kvnw_d, "dA")
            kb.wait(T, "dA")
            kb.op(T, "matmul", PB[6][:, 0:512], ONESR, a32.ap("ckscr", 0, 512, 0, 1),
                  start=True, stop=True, inc=("t", 1))
            kb.wait(Sc, "t")
            kb.op(Sc, "copy", out=a32.ap("wbcast"), in_=PB[6][:, 0:512], inc=("s", 1))

            xap = lambda sub: a32.ap("xts", sub * 2048, (sub + 1) * 2048)
            for st in range(8):
                if st > 0:
                    kb.wait(Sy, "t")
                for sub in range(4):
                    r0 = st * 512 + sub * 128
                    kb.dma(Sy, xap(sub), xfull_d[r0:r0 + 128, :], "dA")
                kb.wait(V, "dA")
                emit_ln_super([xap(i) for i in range(4)], [xap(i) for i in range(4)],
                              D, 1e-5, a32.ap("lnscr"))
                for sub in range(4):
                    tile = st * 4 + sub
                    kb.wait(T, "s")
                    kb.wait(T, "v")
                    for ic in range(16):
                        par = ic % 2
                        lxT = a32.ap("lnxTt", par * 128, (par + 1) * 128)
                        wpar = ic % 2
                        wc = a32.ap("waTc", wpar * 576, wpar * 576 + 576)
                        g = len(TCT["kva"])
                        if g >= 2:
                            kb.wait(Sy, "t", TCT["kva"][g - 2])
                        kb.dma(Sy, wc, waT_d[ic * 128:(ic + 1) * 128, :], f"dA{wpar}")
                        if ic >= 2:
                            kb.wait(T, "s")
                        kb.op(T, "transpose", PB[4 + par][0:128, 0:128],
                              xap(sub)[:, ic * 128:(ic + 1) * 128], IDF, inc=("t", 1))
                        kb.wait(Sc, "t")
                        kb.op(Sc, "copy", out=lxT, in_=PB[4 + par][0:128, 0:128],
                              inc=("s", 1))
                        kb.wait(T, "s")
                        kb.wait(T, f"dA{wpar}")
                        kb.op(T, "matmul", PB[0][:, 0:512], lxT, wc[:, 0:512],
                              start=(ic == 0), stop=(ic == 15), inc=("t", 1))
                        t1 = kb.op(T, "matmul", PB[1][:, 0:64], lxT, wc[:, 512:576],
                                   start=(ic == 0), stop=(ic == 15), inc=("t", 1))
                        TCT["kva"].append(t1)
                    kb.wait(Sc, "t")
                    kb.op(Sc, "activation", out=a32.ap("ckscr"), in_=PB[0][:, 0:512],
                          func=AF.Square, accum_out=LC(36), inc=("s", 1))
                    kb.wait(V, "s")
                    kb.op(V, "tensor_single_scalar", out=LC(37), in_=LC(36),
                          scalar=1.0 / RANK, op=ALU.mult, inc=("v", 1))
                    kb.wait(Sc, "v")
                    kb.op(Sc, "activation", out=LC(38), in_=LC(37), func=AF.Sqrt,
                          bias=cst(1e-6), inc=("s", 1))
                    kb.wait(V, "s")
                    kb.op(V, "reciprocal", out=LC(39), in_=LC(38), inc=("v", 1))
                    kb.wait(Sc, "v")
                    kb.op(Sc, "activation", out=a32.ap("ckscr"), in_=PB[0][:, 0:512],
                          func=AF.Identity, scale=LC(39), inc=("s", 1))
                    kb.wait(V, "s")
                    kb.op(V, "tensor_mul", out=a32.ap("ckscr"), in0=a32.ap("ckscr"),
                          in1=a32.ap("wbcast"))
                    ck = lambda lo, hi: a32.ap("cosk", tile * 64 + lo, tile * 64 + hi)
                    sk = lambda lo, hi: a32.ap("sink", tile * 64 + lo, tile * 64 + hi)
                    kt = a32.ap("cktmp", 0, 64)
                    kb.op(V, "tensor_mul", out=kt[:, 0:32], in0=PB[1][:, 0:32], in1=ck(0, 32))
                    kb.op(V, "tensor_mul", out=kt[:, 32:64], in0=PB[1][:, 32:64], in1=ck(32, 64))
                    kb.op(V, "tensor_mul", out=a32.ap("cktmp", 64, 96), in0=PB[1][:, 32:64],
                          in1=sk(0, 32))
                    kb.op(V, "tensor_mul", out=a32.ap("cktmp", 96, 128), in0=PB[1][:, 0:32],
                          in1=sk(32, 64))
                    kb.op(V, "tensor_sub", out=kt[:, 0:32], in0=kt[:, 0:32],
                          in1=a32.ap("cktmp", 64, 96))
                    kb.op(V, "tensor_add", out=kt[:, 32:64], in0=kt[:, 32:64],
                          in1=a32.ap("cktmp", 96, 128), inc=("v", 1))
                    if st == 0 and sub == 0 and "kt0" in dbg:
                        dump("coskP1", a32.ap("cosk"), after=("s",))
                        dump("sinkP1", a32.ap("sink"), after=("s",))
                        kb.wait(Sc, "v")
                        kb.op(Sc, "copy", out=a32.ap("lnscr", 0, 64), in_=a32.ap("cktmp", 0, 64))
                        kb.op(Sc, "copy", out=a32.ap("lnscr", 64, 128), in_=PB[1][:, 0:64],
                              inc=("s", 1))
                        dump("kt0", a32.ap("lnscr", 0, 64), after=("s",))
                        dump("kpe0", a32.ap("lnscr", 64, 128), after=("s",))
                        kb.wait(V, "dD")
                        kb.wait(T, "dD")
                    tok0 = st * 512 + sub * 128
                    transpose_evac(
                        [a32.ap("ckscr", rc * 128, (rc + 1) * 128) for rc in range(4)] + [kt],
                        [a32.ap("ckvT", rc * 4096 + tok0, rc * 4096 + tok0 + 128)
                         for rc in range(4)] + [a32.ap("kp", tok0, tok0 + 128, 0, 64)],
                        IDF)
            a32.free("wbcast", "xts", "lnscr", "lnxTt", "waTc", "ckscr", "cktmp")

        if "P1" in phases:
            emit_p1()
            kb.barrier()
            dump("ckvT", a32.ap("ckvT"), after=("s", "v"))
            dump("kpeT", a32.ap("kp", 0, 4096, 0, 64), after=("s", "v"))

        def emit_p2():
            a32.free("cosk", "sink")
            a32.alloc("xts2", 4 * 2048)
            a32.alloc("lnscr2", 2048)
            a32.alloc("lnxT", 16 * 512)
            a32.alloc("wqc", 2 * 384)
            a32.alloc("qstg", 512)
            xap = lambda sub: a32.ap("xts2", sub * 2048, (sub + 1) * 2048)
            qmm = []
            for st in range(4):
                kb.wait(Sy, "t")
                kb.wait(Sy, "s")
                for sub in range(4):
                    r0 = st * 512 + sub * 128
                    kb.dma(Sy, xap(sub), xown_d[r0:r0 + 128, :], "dA")
                kb.wait(V, "dA")
                emit_ln_super([xap(i) for i in range(4)], [xap(i) for i in range(4)],
                              D, 1e-5, a32.ap("lnscr2"))
                srcs, dsts = [], []
                for ic in range(16):
                    for sub in range(4):
                        srcs.append(xap(sub)[:, ic * 128:(ic + 1) * 128])
                        dsts.append(a32.ap("lnxT", ic * 512 + sub * 128,
                                           ic * 512 + sub * 128 + 128))
                transpose_evac(srcs, dsts, IDF)
                for grp in range(8):
                    h0, h1 = 2 * grp, 2 * grp + 1
                    kb.wait(T, "s")
                    kb.wait(T, "v")
                    for ic in range(16):
                        par = ic % 2
                        wc = a32.ap("wqc", par * 384, (par + 1) * 384)
                        g = len(qmm)
                        if g >= 2:
                            kb.wait(Sy, "t", qmm[g - 2])
                        kb.dma(Sy, wc, wqT_d[ic * 128:(ic + 1) * 128,
                                             grp * 384:(grp + 1) * 384], f"dA{par}")
                        kb.wait(T, f"dA{par}")
                        kb.op(T, "matmul", PB[0][:, 0:512], wc[:, 0:128],
                              a32.ap("lnxT", ic * 512, (ic + 1) * 512),
                              start=(ic == 0), stop=(ic == 15), inc=("t", 1))
                        kb.op(T, "matmul", PB[2][0:64, 0:512], wc[:, 128:192],
                              a32.ap("lnxT", ic * 512, (ic + 1) * 512),
                              start=(ic == 0), stop=(ic == 15), inc=("t", 1))
                        kb.op(T, "matmul", PB[1][:, 0:512], wc[:, 192:320],
                              a32.ap("lnxT", ic * 512, (ic + 1) * 512),
                              start=(ic == 0), stop=(ic == 15), inc=("t", 1))
                        t1 = kb.op(T, "matmul", PB[3][0:64, 0:512], wc[:, 320:384],
                                   a32.ap("lnxT", ic * 512, (ic + 1) * 512),
                                   start=(ic == 0), stop=(ic == 15), inc=("t", 1))
                        qmm.append(t1)
                    kb.wait(Sc, "t")
                    kb.wait(V, "t")
                    cq = lambda r0, r1: a32.ap("cosq", st * 512, st * 512 + 512, r0, r1)
                    sq = lambda r0, r1: a32.ap("sinq", st * 512, st * 512 + 512, r0, r1)
                    for hi, (pbn, pbr) in enumerate(((0, 2), (1, 3))):
                        h = h0 if hi == 0 else h1
                        kb.wait(Sc, "dA")
                        kb.wait(V, "dA")
                        qs = a32.ap("lnscr2", 0, 512)
                        qr = a32.ap("qstg", 0, 512, 0, 64)
                        qpe = a32.ap("lnscr2", 512, 1024, 0, 64)
                        qsw = a32.ap("lnscr2", 1024, 1536)
                        qt = a32.ap("lnscr2", 1536, 2048, 0, 64)
                        kb.op(Sc, "copy", out=qs, in_=PB[pbn][:, 0:512])
                        kb.op(Sc, "copy", out=qpe, in_=PB[pbr][0:64, 0:512], inc=("s", 1))
                        kb.wait(T, "s")
                        kb.wait(T, "v")
                        kb.op(T, "matmul", PB[4][0:64, 0:512],
                              a32.ap("swap64", 0, 64, 0, 64), qpe,
                              start=True, stop=True, inc=("t", 1))
                        kb.wait(V, "t")
                        kb.op(V, "tensor_mul", out=qr, in0=PB[pbr][0:64, 0:512],
                              in1=cq(0, 64))
                        kb.op(V, "tensor_mul", out=qt, in0=PB[4][0:64, 0:512],
                              in1=sq(0, 64))
                        kb.op(V, "tensor_add", out=qr, in0=qr, in1=qt, inc=("v", 1))
                        if st == 0 and grp == 0 and hi == 0 and "qr0" in dbg:
                            dump("qr0", a32.ap("qstg", 0, 512), after=("v",))
                            kb.op(Sc, "copy", out=a32.ap("lnscr2", 512, 1024, 0, 64),
                                  in_=PB[pbr][0:64, 0:512], inc=("s", 1))
                            dump("qpe0", a32.ap("lnscr2", 512, 1024, 0, 64), after=("s",))
                            kb.wait(V, "dD")
                            kb.wait(T, "dD")
                            kb.wait(Sc, "dD")
                        kb.wait(Sy, "s")
                        kb.wait(Sy, "v")
                        kb.dma(Sy, qsp_d[h * QHD:h * QHD + 128, st * 512:(st + 1) * 512],
                               qs, "dA")
                        kb.dma(Sy, qsp_d[h * QHD + 128:h * QHD + 192,
                                         st * 512:(st + 1) * 512], qr, "dA")
            a32.free("xts2", "lnscr2", "lnxT", "wqc", "qstg")

        if "P2" in phases:
            emit_p2()
            kb.barrier()

        def emit_p3():
            a32.free("cosq", "sinq")
            a32.alloc("qro", 2048)
            a32.alloc("knT", 4096)
            a32.alloc("vsb", 4096)
            a32.alloc("qn", 2048)
            a32.alloc("pbuf", 2 * 4096)
            a32.alloc("aoT", 2048)
            a32.alloc("wkb", 1024)
            a32.alloc("msk", 512)
            a32.alloc("ptb0", 256)
            a32.alloc("ptb1", 256)
            for h in range(p3_heads):
                kb.wait(Sy, "t")
                kb.wait(Sy, "s")
                kb.wait(Sy, "v")
                kb.wait(Sc, "dB")
                kb.dma(Sy, a32.ap("qn"), qsp_d[h * QHD:h * QHD + 128, :], "dB")
                kb.dma(Sy, a32.ap("qro", 0, 2048, 0, 64),
                       qsp_d[h * QHD + 128:h * QHD + 192, :], "dB")
                for rc in range(4):
                    kb.dma(Sy, a32.ap("wkb", rc * 256, (rc + 1) * 256),
                           wkvbT_d[rc * 128:(rc + 1) * 128,
                                   h * 256:(h + 1) * 256], "dB")
                kb.wait(T, "dB")
                knev = []
                for kt in range(8):
                    pb = PB[kt % 2]
                    if kt >= 2:
                        kb.wait(T, "s", knev[kt - 2])
                    for rc in range(4):
                        kb.op(T, "matmul", pb[:, 0:512],
                              a32.ap("wkb", rc * 256, rc * 256 + 128),
                              a32.ap("ckvT", rc * 4096 + kt * 512, rc * 4096 + (kt + 1) * 512),
                              start=(rc == 0), stop=(rc == 3), inc=("t", 1))
                    kb.wait(Sc, "t")
                    knev.append(kb.op(Sc, "copy", out=a32.ap("knT", kt * 512, (kt + 1) * 512),
                          in_=pb[:, 0:512], inc=("s", 1)))
                vev = []
                for ktc in range(32):
                    pb = PB[2 + ktc % 2]
                    if ktc >= 2:
                        kb.wait(T, "s", vev[ktc - 2])
                    for rc in range(4):
                        kb.op(T, "matmul", pb[0:128, 0:128],
                              a32.ap("ckvT", rc * 4096 + ktc * 128, rc * 4096 + (ktc + 1) * 128),
                              a32.ap("wkb", rc * 256 + 128, (rc + 1) * 256),
                              start=(rc == 0), stop=(rc == 3), inc=("t", 1))
                    kb.wait(Sc, "t")
                    vev.append(kb.op(Sc, "copy", out=a32.ap("vsb", ktc * 128, (ktc + 1) * 128),
                          in_=pb[0:128, 0:128], inc=("s", 1)))
                for j in range(8):
                    kext = (j + 1) * 512
                    for ss in range(2):
                        qc0 = j * 256 + ss * 128
                        kb.wait(G, "v")
                        kb.wait(G, "s")
                        kb.op(G, "iota", a32.ap("msk"),
                              [[1, 512]], base=j * 512, channel_multiplier=0,
                              allow_small_or_imprecise_dtypes=True, inc=("gp", 1))
                        kb.wait(V, "gp")
                        kb.op(V, "tensor_scalar", out=a32.ap("msk"),
                              in0=a32.ap("msk"),
                              scalar1=a32.ap("qpos", 2 * j + ss, 2 * j + ss + 1),
                              scalar2=NEG, op0=ALU.is_gt, op1=ALU.mult, inc=("v", 1))
                        kb.wait(T, "s")
                        kb.wait(T, "v")
                        scev = []
                        for kt in range(j + 1):
                            pb = PB[kt % 2]
                            if kt >= 2:
                                sem_nm, val = scev[kt - 2]
                                kb.wait(T, sem_nm, val)
                            kb.op(T, "matmul", pb[:, 0:512],
                                  a32.ap("qn")[:, qc0:qc0 + 128],
                                  a32.ap("knT", kt * 512, (kt + 1) * 512),
                                  start=True, stop=False, inc=("t", 1))
                            kb.op(T, "matmul", pb[:, 0:512],
                                  a32.ap("qro", qc0, qc0 + 128, 0, 64),
                                  a32.ap("kp", kt * 512, (kt + 1) * 512, 0, 64),
                                  start=False, stop=True, inc=("t", 1))
                            dst = a32.ap("pbuf", ss * 4096 + kt * 512,
                                         ss * 4096 + (kt + 1) * 512)
                            if kt == j:
                                kb.wait(V, "t")
                                scev.append(("v", kb.op(V, "tensor_add", out=dst,
                                      in0=pb[:, 0:512],
                                      in1=a32.ap("msk"), inc=("v", 1))))
                            else:
                                kb.wait(Sc, "t")
                                scev.append(("s", kb.op(Sc, "copy", out=dst,
                                      in_=pb[:, 0:512], inc=("s", 1))))
                        P = a32.ap("pbuf", ss * 4096, ss * 4096 + kext)
                        kb.wait(V, "s")
                        kb.op(V, "reduce_max", out=LC(36), in_=P, axis=AX.X, inc=("v", 1))
                        kb.wait(Sc, "v")
                        kb.op(Sc, "copy", out=LC(35), in_=LC(36), inc=("s", 1))
                        kb.wait(V, "s")
                        kb.op(V, "tensor_single_scalar", out=LC(37), in_=LC(35),
                              scalar=-SCALE, op=ALU.mult, inc=("v", 1))
                        kb.wait(Sc, "v")
                        kb.op(Sc, "activation", out=P, in_=P, func=AF.Exp, scale=SCALE,
                              bias=LC(37), accum_out=LC(38), inc=("s", 1))
                        kb.wait(V, "s")
                        kb.op(V, "reciprocal", out=LC(39), in_=LC(38), inc=("v", 1))
                        kb.wait(Sc, "v")
                        kb.op(Sc, "activation", out=P, in_=P, func=AF.Identity,
                              scale=LC(39), inc=("s", 1))
                    kb.wait(T, "s")
                    nk = (j + 1) * 4
                    ptev = []
                    pvmm = []
                    for kc in range(nk):
                        kpar = kc % 2
                        for ss in range(2):
                            bank = PB[4 + (2 * kc + ss) % 2]
                            if 2 * kc + ss >= 2:
                                kb.wait(T, "s", ptev[2 * kc + ss - 2])
                            if ss == 0 and kc >= 2:
                                kb.wait(T, "t", pvmm[kc - 2])
                            kb.op(T, "transpose", bank[0:128, 0:128],
                                  a32.ap("pbuf", ss * 4096 + kc * 128,
                                         ss * 4096 + (kc + 1) * 128), IDF,
                                  inc=("t", 1))
                            kb.wait(Sc, "t")
                            if ss == 0 and kc >= 2:
                                kb.wait(Sc, "t", pvmm[kc - 2])
                            ptev.append(kb.op(Sc, "copy",
                                  out=a32.ap(f"ptb{kpar}", ss * 128, (ss + 1) * 128),
                                  in_=bank[0:128, 0:128], inc=("s", 1)))
                        kb.wait(T, "s", ptev[-1])
                        pvmm.append(kb.op(T, "matmul", PB[6][:, 0:256],
                              a32.ap("vsb", kc * 128, (kc + 1) * 128),
                              a32.ap(f"ptb{kpar}"),
                              start=(kc == 0), stop=(kc == nk - 1), inc=("t", 1)))
                    kb.wait(Sc, "t")
                    kb.op(Sc, "copy", out=a32.ap("aoT", j * 256, (j + 1) * 256),
                          in_=PB[6][:, 0:256], inc=("s", 1))
                kb.wait(Sy, "s")
                kb.dma(Sy, aosp_d[h * VD:(h + 1) * VD, :], a32.ap("aoT"), "dB")
            kb.barrier()
            a32.free("qro", "knT", "vsb", "qn", "pbuf", "ptb0", "ptb1", "aoT", "wkb", "msk")

        if "P3" in phases:
            emit_p3()
        def emit_quant_cols(src_ap, w, g_ap, inv_col, out_bf, tag, dbg_tag=None):
            rb = lambda i: a32.ap("rscr", i * 2048, i * 2048 + w)
            ii = a32.ap("rscr", 1 * 2048, 1 * 2048 + w).bitcast(I32)
            kb.wait(Sc, "v")
            kb.op(Sc, "activation", out=rb(0), in_=src_ap, func=AF.Identity,
                  scale=inv_col, inc=("s", 1))
            kb.wait(V, "s")
            kb.op(V, "tensor_mul", out=rb(0), in0=rb(0), in1=g_ap)
            kb.op(V, "reduce_max", out=LC(20), in_=rb(0), axis=AX.X,
                  apply_absolute_value=True, inc=("v", 1))
            kb.wait(Sc, "v")
            kb.op(Sc, "copy", out=LC(21), in_=LC(20), inc=("s", 1))
            kb.wait(V, "s")
            kb.op(V, "tensor_single_scalar", out=LC(22), in_=LC(21), scalar=1e-4,
                  op=ALU.max, inc=("v", 1))
            kb.wait(Sc, "v")
            kb.op(Sc, "copy", out=LC(23), in_=LC(22), inc=("s", 1))
            kb.wait(V, "s")
            kb.op(V, "reciprocal", out=LC(24), in_=LC(23), inc=("v", 1))
            kb.wait(Sc, "v")
            kb.op(Sc, "copy", out=LC(34), in_=LC(24), inc=("s", 1))
            kb.wait(V, "s")
            kb.op(V, "tensor_single_scalar", out=LC(25), in_=LC(34), scalar=127.0,
                  op=ALU.mult, inc=("v", 1))
            kb.wait(Sc, "v")
            kb.op(Sc, "activation", out=rb(0), in_=rb(0), func=AF.Identity,
                  scale=LC(25), inc=("s", 1))
            kb.wait(V, "s")
            if dbg_tag and f"{dbg_tag}_y" in dbg:
                dump(f"{dbg_tag}_y", rb(0), after=("s",))
                dump(f"{dbg_tag}_am", a32.ap("lncol", 16, 34), after=("s", "v"))
                kb.wait(V, "dD")
                kb.wait(Sc, "dD")
                kb.wait(G, "dD")
            kb.op(V, "engine_nop", inc=("v", 1))
            emit_round(rb(0), rb(2), ii, rb(1), a32.ap("aoc", 0, w), vsem="v", gsem="gd")
            kb.op(V, "engine_nop", inc=("v", 1))
            kb.wait(Sc, "v")
            kb.wait(Sc, "t")
            kb.op(Sc, "copy", out=out_bf, in_=rb(2), inc=("s", 1))
            return a32.ap("lncol", 23, 24)

        EMM = []
        Y3MM = []

        def emit_p4():
            a32.free("ckvT", "kp")
            a32.alloc("aoc", 4096)
            a32.alloc("woc", 2 * 2048)
            a32.alloc("resid", 2 * 2048)
            a32.alloc("ln2", 2 * 2048)
            a32.alloc("rscr", 3 * 2048)
            a32.alloc("g1b", 2048)
            a32.alloc("g2b", 2048)
            a32.alloc("g3b", 8192)
            a32.alloc("outacc", 2 * 2048)
            a32.alloc("grow", 2048)
            a16.alloc("kT1", 4096)
            a16.alloc("kT2", 4096)
            a16.alloc("kbf", 2048)
            a16.alloc("k3Ts", 2048)
            a32.alloc("tstrf", 2 * 1536)
            def tstr(lo, hi):
                return a32.ap("tstrf", lo // 2, hi // 2).bitcast(BF16)
            for gd_, dst, w in ((g1_d, "g1b", D), (g2_d, "g2b", D), (g3_d, "g3b", DH)):
                for seg in range(w // 2048):
                    kb.wait(Sy, "s")
                    kb.wait(Sy, "t")
                    kb.dma(Sy, a32.ap("grow", 0, 2048, 0, 1),
                           gd_[0:1, seg * 2048:(seg + 1) * 2048], "dC")
                    kb.wait(T, "dC")
                    for c in range(4):
                        kb.wait(T, "s")
                        kb.op(T, "matmul", PB[c % 2][:, 0:512], ONESR,
                              a32.ap("grow", c * 512, (c + 1) * 512, 0, 1),
                              start=True, stop=True, inc=("t", 1))
                        kb.wait(Sc, "t")
                        kb.op(Sc, "copy",
                              out=a32.ap(dst, seg * 2048 + c * 512,
                                         seg * 2048 + (c + 1) * 512),
                              in_=PB[c % 2][:, 0:512], inc=("s", 1))
                kb.wait(T, "s")
            for tt in range(p4_tiles):
                r0 = tt * 256
                kb.wait(Sy, "t")
                kb.wait(Sy, "s")
                kb.wait(Sy, "v")
                for hc in range(16):
                    kb.dma(Sy, a32.ap("aoc", hc * 256, (hc + 1) * 256),
                           aosp_d[hc * 128:(hc + 1) * 128, r0:r0 + 256], "dC")
                for tc in range(2):
                    kb.dma(Sy, a32.ap("ln2", tc * 2048, (tc + 1) * 2048),
                           xown_d[r0 + tc * 128:r0 + (tc + 1) * 128, :], "dC")
                wmm = []
                for hc in range(16):
                    par = hc % 2
                    if hc >= 2:
                        kb.wait(Sy, "t", wmm[hc - 2])
                    kb.dma(Sy, a32.ap("woc", par * 2048, (par + 1) * 2048),
                           woT_d[hc * 128:(hc + 1) * 128, :], f"dW{par}")
                    kb.wait(T, f"dW{par}")
                    if hc == 0:
                        kb.wait(T, "s")
                        kb.wait(T, "v")
                    last = None
                    for tc in range(2):
                        for dc in range(4):
                            last = kb.op(T, "matmul", PB[tc * 4 + dc][:, 0:512],
                                         a32.ap("aoc", hc * 256 + tc * 128,
                                                hc * 256 + (tc + 1) * 128),
                                         a32.ap("woc", par * 2048 + dc * 512,
                                                par * 2048 + (dc + 1) * 512),
                                         start=(hc == 0), stop=(hc == 15),
                                         inc=("t", 1))
                    wmm.append(last)
                kb.wait(V, "t")
                kb.wait(V, "dC")
                for tc in range(2):
                    for dc in range(4):
                        kb.op(V, "tensor_add",
                              out=a32.ap("resid", tc * 2048 + dc * 512,
                                         tc * 2048 + (dc + 1) * 512),
                              in0=PB[tc * 4 + dc][:, 0:512],
                              in1=a32.ap("ln2", tc * 2048 + dc * 512,
                                         tc * 2048 + (dc + 1) * 512))
                kb.op(V, "engine_nop", inc=("v", 1))
                if tt == 0 and "resid0" in dbg:
                    dump("resid0", a32.ap("resid"), after=("v",))
                    kb.wait(Sc, "dD")
                    kb.wait(T, "dD")
                kb.wait(Sc, "v")
                rap = lambda tc: a32.ap("resid", tc * 2048, (tc + 1) * 2048)
                lap = lambda tc: a32.ap("ln2", tc * 2048, (tc + 1) * 2048)
                emit_ln_super([rap(0), rap(1)], [lap(0), lap(1)], D, 1e-5,
                              a32.ap("rscr", 0, 2048))
                if tt == 0 and "ln20" in dbg:
                    dump("ln20", a32.ap("ln2"), after=("s",))
                    kb.wait(Sc, "dD")
                    kb.wait(V, "dD")
                    kb.wait(T, "dD")
                for tc in range(2):
                    kb.op(Sc, "activation", out=a32.ap("rscr", 0, 2048), in_=lap(tc),
                          func=AF.Square, accum_out=LC(16), inc=("s", 1))
                    kb.wait(V, "s")
                    kb.op(V, "tensor_single_scalar", out=LC(17), in_=LC(16),
                          scalar=1.0 / D, op=ALU.mult, inc=("v", 1))
                    kb.wait(Sc, "v")
                    kb.op(Sc, "activation", out=LC(18), in_=LC(17), func=AF.Sqrt,
                          bias=cst(1e-8), inc=("s", 1))
                    kb.wait(V, "s")
                    kb.op(V, "reciprocal", out=LC(19), in_=LC(18), inc=("v", 1))
                    for j, (gb, kT) in enumerate((("g1b", "kT1"), ("g2b", "kT2"))):
                        amaxc = emit_quant_cols(lap(tc), D, a32.ap(gb), LC(19),
                                                a16.ap("kbf"), f"q{j}",
                                                dbg_tag=(f"t{tc}j{j}" if tt == 0 else None))
                        kb.wait(V, "s")
                        kb.op(V, "scalar_tensor_tensor", out=LC(26 + 2 * tc + j),
                              in0=amaxc, scalar=1.0 / 127.0,
                              in1=a32.ap("acol", j + 1, j + 2),
                              op0=ALU.mult, op1=ALU.mult, inc=("v", 1))
                        kb.wait(T, "s")
                        kb.wait(T, "v")
                        kev = []
                        for fc in range(16):
                            bank = PB[fc % 2]
                            if fc >= 2:
                                kb.wait(T, "s", kev[fc - 2])
                            kb.op(T, "transpose", bank[0:128, 0:64].bitcast(BF16),
                                  a16.ap("kbf", fc * 128, (fc + 1) * 128), IDB,
                                  inc=("t", 1))
                            kb.wait(Sc, "t")
                            kev.append(kb.op(Sc, "copy",
                                  out=a16.ap(kT, fc * 256 + tc * 128,
                                             fc * 256 + (tc + 1) * 128),
                                  in_=bank[0:128, 0:64].bitcast(BF16), inc=("s", 1)))
                if tt == 0 and "kt10" in dbg:
                    dump("kt10", a16.ap("kT1"), after=("s",))
                    kb.wait(Sc, "dD")
                    kb.wait(T, "dD")
                CPL = 6
                loads = [(0, 6), (6, 12), (12, 16)]
                kb.wait(Sy, "t")
                epi = []
                for fs in range(16):
                    fpar = fs % 2
                    if fs >= 2:
                        kb.wait(T, "s", epi[fs - 2][0])
                        kb.wait(T, "v", epi[fs - 2][1])
                    else:
                        kb.wait(T, "s")
                        kb.wait(T, "v")
                    for wj, (tch, kTn) in enumerate(((t1c_d, "kT1"), (t2c_d, "kT2"))):
                        for li, (f0, f1) in enumerate(loads):
                            par = (len(loads) * wj + li) % 2
                            g = len(EMM)
                            if g >= 2:
                                kb.wait(Sy, "t", EMM[g - 2])
                            for fc in range(f0, f1):
                                kb.dma(Sy, tstr(par * 3072 + (fc - f0) * 512,
                                                par * 3072 + (fc - f0 + 1) * 512),
                                       tch[fc, :, fs * 512:(fs + 1) * 512], f"dE{par}")
                            kb.wait(T, f"dE{par}")
                            last = None
                            for fc in range(f0, f1):
                                for tc in range(2):
                                    last = kb.op(T, "matmul",
                                          PB[4 * fpar + 2 * wj + tc][:, 0:512],
                                          a16.ap(kTn, fc * 256 + tc * 128,
                                                 fc * 256 + (tc + 1) * 128),
                                          tstr(par * 3072 + (fc - f0) * 512,
                                                 par * 3072 + (fc - f0 + 1) * 512),
                                          start=(fc == 0), stop=(fc == 15),
                                          inc=("t", 1))
                            EMM.append(last)
                    kb.wait(Sc, "t")
                    kb.wait(Sc, "v")
                    kb.wait(V, "t")
                    kb.wait(V, "g")
                    if tt == 0 and fs == 0 and "y1p" in dbg:
                        kb.op(Sc, "copy", out=a32.ap("rscr", 4096, 4608),
                              in_=PB[0][:, 0:512])
                        kb.op(Sc, "copy", out=a32.ap("rscr", 4608, 5120),
                              in_=PB[2][:, 0:512], inc=("s", 1))
                        dump("y1p", a32.ap("rscr", 4096, 4608), after=("s",))
                        dump("y2p", a32.ap("rscr", 4608, 5120), after=("s",))
                        dump("mcols", a32.ap("lncol", 26, 33), after=("s",))
                        kb.wait(Sc, "dD")
                        kb.wait(V, "dD")
                        kb.wait(T, "dD")
                    sN = vN = None
                    for tc in range(2):
                        sN = kb.op(Sc, "activation",
                              out=a32.ap("rscr", tc * 512, tc * 512 + 512),
                              in_=PB[4 * fpar + tc][:, 0:512], func=AF.Silu,
                              scale=LC(26 + 2 * tc + 0), inc=("s", 1))
                        kb.wait(V, "s")
                        vN = kb.op(V, "scalar_tensor_tensor",
                              out=a32.ap("rscr", 1024 + tc * 512, 1024 + (tc + 1) * 512),
                              in0=PB[4 * fpar + 2 + tc][:, 0:512], scalar=LC(26 + 2 * tc + 1),
                              in1=a32.ap("rscr", tc * 512, tc * 512 + 512),
                              op0=ALU.mult, op1=ALU.mult, inc=("v", 1))
                        kb.wait(G, "v")
                        kb.dma(G, hsp_d[tc, :, fs * 512:(fs + 1) * 512],
                               a32.ap("rscr", 1024 + tc * 512, 1024 + (tc + 1) * 512), "g")
                    epi.append((sN, vN))
                if tt == 0 and "h0" in dbg:
                    kb.wait(Sy, "g")
                    kb.wait(Sy, "v")
                    kb.wait(Sy, "s")
                    kb.dma(Sy, a32.ap("rscr", 0, 2048), hsp_d[0, :, 0:2048], "dC")
                    kb.wait(Sy, "dC")
                    kb.dma(Sy, dbg["h0"], a32.ap("rscr", 0, 2048), "dD")
                    kb.wait(Sy, "dD")
                    kb.wait(Sc, "dD")
                    kb.wait(V, "dD")
                for tc in range(2):
                    kb.wait(Sy, "t")
                    kb.wait(Sy, "g")
                    kb.wait(Sy, "v")
                    kb.wait(Sy, "s")
                    sqev = []
                    for sl in range(4):
                        par = sl % 2
                        if sl >= 2:
                            kb.wait(Sy, "s", sqev[sl - 2])
                        kb.dma(Sy, a32.ap("rscr", par * 2048, (par + 1) * 2048),
                               hsp_d[tc, :, sl * 2048:(sl + 1) * 2048], f"dH{par}")
                        kb.wait(Sc, f"dH{par}")
                        sqev.append(kb.op(Sc, "activation", out=a32.ap("rscr", 4096, 6144),
                              in_=a32.ap("rscr", par * 2048, (par + 1) * 2048),
                              func=AF.Square,
                              accum_out=LC(16 + sl), inc=("s", 1)))
                    kb.wait(V, "s")
                    kb.op(V, "tensor_add", out=LC(16), in0=LC(16), in1=LC(17))
                    kb.op(V, "tensor_add", out=LC(18), in0=LC(18), in1=LC(19), inc=("v", 1))
                    kb.wait(Sc, "v")
                    kb.op(Sc, "copy", out=LC(20), in_=LC(16))
                    kb.op(Sc, "copy", out=LC(21), in_=LC(18), inc=("s", 1))
                    kb.wait(V, "s")
                    kb.op(V, "tensor_add", out=LC(22), in0=LC(20), in1=LC(21), inc=("v", 1))
                    kb.wait(Sc, "v")
                    kb.op(Sc, "activation", out=LC(23), in_=LC(22), func=AF.Sqrt,
                          scale=1.0 / DH, bias=cst(1e-8), inc=("s", 1))
                    kb.wait(V, "s")
                    kb.op(V, "reciprocal", out=LC(24), in_=LC(23), inc=("v", 1))
                    kb.wait(Sc, "v")
                    kb.wait(Sy, "v")
                    amev = []
                    for sl in range(4):
                        par = sl % 2
                        if sl >= 2:
                            kb.wait(Sy, "s", amev[sl - 2])
                        kb.dma(Sy, a32.ap("rscr", par * 2048, (par + 1) * 2048),
                               hsp_d[tc, :, sl * 2048:(sl + 1) * 2048], f"dH{par}")
                        kb.wait(Sc, f"dH{par}")
                        kb.wait(Sc, "v")
                        amev.append(kb.op(Sc, "activation", out=a32.ap("rscr", 4096, 6144),
                              in_=a32.ap("rscr", par * 2048, (par + 1) * 2048),
                              func=AF.Identity,
                              scale=LC(24), inc=("s", 1)))
                        kb.wait(V, "s")
                        kb.op(V, "tensor_mul", out=a32.ap("rscr", 4096, 6144),
                              in0=a32.ap("rscr", 4096, 6144),
                              in1=a32.ap("g3b", sl * 2048, (sl + 1) * 2048))
                        kb.op(V, "reduce_max", out=LC(16 + sl),
                              in_=a32.ap("rscr", 4096, 6144), axis=AX.X,
                              apply_absolute_value=True, inc=("v", 1))
                    kb.wait(Sc, "v")
                    kb.op(Sc, "copy", out=LC(20), in_=LC(16))
                    kb.op(Sc, "copy", out=LC(21), in_=LC(18), inc=("s", 1))
                    kb.wait(V, "s")
                    kb.op(V, "tensor_max", out=LC(20), in0=LC(20), in1=LC(17))
                    kb.op(V, "tensor_max", out=LC(21), in0=LC(21), in1=LC(19), inc=("v", 1))
                    kb.wait(Sc, "v")
                    kb.op(Sc, "copy", out=LC(22), in_=LC(20), inc=("s", 1))
                    kb.wait(V, "s")
                    kb.op(V, "tensor_max", out=LC(22), in0=LC(22), in1=LC(21),
                          inc=("v", 1))
                    kb.wait(Sc, "v")
                    kb.op(Sc, "copy", out=LC(36), in_=LC(22), inc=("s", 1))
                    kb.wait(V, "s")
                    kb.op(V, "tensor_single_scalar", out=LC(23), in_=LC(36),
                          scalar=1e-4, op=ALU.max, inc=("v", 1))
                    kb.wait(Sc, "v")
                    kb.op(Sc, "copy", out=LC(35), in_=LC(23), inc=("s", 1))
                    kb.wait(V, "s")
                    kb.op(V, "reciprocal", out=LC(25), in_=LC(35), inc=("v", 1))
                    kb.wait(Sc, "v")
                    kb.op(Sc, "copy", out=LC(34), in_=LC(25), inc=("s", 1))
                    kb.wait(V, "s")
                    kb.op(V, "tensor_single_scalar", out=LC(31), in_=LC(34),
                          scalar=127.0, op=ALU.mult)
                    kb.op(V, "scalar_tensor_tensor", out=LC(32), in0=LC(23),
                          scalar=1.0 / 127.0, in1=a32.ap("acol", 3, 4),
                          op0=ALU.mult, op1=ALU.mult, inc=("v", 1))
                    kb.wait(Sc, "v")
                    for sl in range(4):
                        kb.wait(Sy, "v")
                        kb.wait(Sy, "gd")
                        kb.dma(Sy, a32.ap("rscr", 0, 2048),
                               hsp_d[tc, :, sl * 2048:(sl + 1) * 2048], "dC")
                        kb.wait(Sc, "dC")
                        kb.wait(Sc, "v")
                        kb.op(Sc, "activation", out=a32.ap("rscr", 0, 2048),
                              in_=a32.ap("rscr", 0, 2048), func=AF.Identity,
                              scale=LC(24), inc=("s", 1))
                        kb.wait(V, "s")
                        kb.op(V, "tensor_mul", out=a32.ap("rscr", 0, 2048),
                              in0=a32.ap("rscr", 0, 2048),
                              in1=a32.ap("g3b", sl * 2048, (sl + 1) * 2048),
                              inc=("v", 1))
                        kb.wait(Sc, "v")
                        kb.op(Sc, "activation", out=a32.ap("rscr", 0, 2048),
                              in_=a32.ap("rscr", 0, 2048), func=AF.Identity,
                              scale=LC(31), inc=("s", 1))
                        kb.wait(V, "s")
                        kb.op(V, "engine_nop", inc=("v", 1))
                        if tt == 0 and tc == 0 and sl == 0 and "y3q0" in dbg:
                            dump("y3q0", a32.ap("rscr", 0, 2048), after=("v", "s"))
                            kb.wait(V, "dD")
                            kb.wait(Sc, "dD")
                            kb.wait(G, "dD")
                            kb.wait(T, "dD")
                        ii3 = a32.ap("rscr", 2048, 4096).bitcast(I32)
                        emit_round(a32.ap("rscr", 0, 2048), a32.ap("rscr", 4096, 6144),
                                   ii3, a32.ap("rscr", 2048, 4096),
                                   a32.ap("aoc", 0, 2048), vsem="v", gsem="gd")
                        kb.op(V, "engine_nop", inc=("v", 1))
                        kb.wait(Sc, "v")
                        kb.op(Sc, "copy", out=a16.ap("kbf"),
                              in_=a32.ap("rscr", 4096, 6144), inc=("s", 1))
                        kb.wait(T, "s")
                        kb.wait(T, "v")
                        k3ev = []
                        for fc in range(16):
                            bank = PB[fc % 2]
                            if fc >= 2:
                                kb.wait(T, "s", k3ev[fc - 2])
                            kb.op(T, "transpose", bank[0:128, 0:64].bitcast(BF16),
                                  a16.ap("kbf", fc * 128, (fc + 1) * 128), IDB,
                                  inc=("t", 1))
                            kb.wait(Sc, "t")
                            k3ev.append(kb.op(Sc, "copy",
                                  out=a16.ap("k3Ts", fc * 128, (fc + 1) * 128),
                                  in_=bank[0:128, 0:64].bitcast(BF16), inc=("s", 1)))
                        kb.wait(T, "s")
                        for dc in range(4):
                            bank = PB[4 + dc % 2]
                            if dc >= 2:
                                kb.wait(T, "v")
                            for fc in range(16):
                                par = fc % 2
                                g = len(Y3MM)
                                if g >= 2:
                                    kb.wait(Sy, "t", Y3MM[g - 2])
                                kb.dma(Sy, tstr(par * 1536, par * 1536 + 512),
                                       t3c_d[sl * 16 + fc, :, dc * 512:(dc + 1) * 512],
                                       f"dY{par}")
                                kb.wait(T, f"dY{par}")
                                Y3MM.append(kb.op(T, "matmul", bank[:, 0:512],
                                      a16.ap("k3Ts", fc * 128, (fc + 1) * 128),
                                      tstr(par * 1536, par * 1536 + 512),
                                      start=(fc == 0), stop=(fc == 15), inc=("t", 1)))
                            kb.wait(V, "t")
                            if sl == 0:
                                kb.op(V, "tensor_copy",
                                      out=a32.ap("outacc", tc * 2048 + dc * 512,
                                                 tc * 2048 + (dc + 1) * 512),
                                      in_=bank[:, 0:512], inc=("v", 1))
                            else:
                                kb.op(V, "tensor_add",
                                      out=a32.ap("outacc", tc * 2048 + dc * 512,
                                                 tc * 2048 + (dc + 1) * 512),
                                      in0=a32.ap("outacc", tc * 2048 + dc * 512,
                                                 tc * 2048 + (dc + 1) * 512),
                                      in1=bank[:, 0:512], inc=("v", 1))
                    if tt == 0 and tc == 0 and "oacc0" in dbg:
                        dump("oacc0", a32.ap("outacc", 0, 2048), after=("v", "s"))
                        dump("qcols", a32.ap("lncol", 16, 35), after=("v", "s"))
                        dump("k3s", a16.ap("k3Ts"), after=("s",))
                        kb.wait(V, "dD")
                        kb.wait(Sc, "dD")
                        kb.wait(T, "dD")
                    kb.wait(V, "s")
                    kb.op(V, "scalar_tensor_tensor", out=a32.ap("outacc", tc * 2048,
                                                                (tc + 1) * 2048),
                          in0=a32.ap("outacc", tc * 2048, (tc + 1) * 2048),
                          scalar=LC(32), in1=rap(tc), op0=ALU.mult, op1=ALU.add,
                          inc=("v", 1))
                    kb.wait(Sy, "v")
                    kb.dma(Sy, out_d[r0 + tc * 128:r0 + (tc + 1) * 128, :],
                           a32.ap("outacc", tc * 2048, (tc + 1) * 2048), "dO")
            kb.wait(Sy, "dO")

        if "P4" in phases:
            emit_p4()
            if "ao0" in dbg:
                a32.alloc("probe3", 2048)
                for nm, h in (("ao0", 0), ("ao9", 9)):
                    kb.wait(Sy, "dB")
                    kb.dma(Sy, a32.ap("probe3"), aosp_d[h * VD:(h + 1) * VD, :], "dB")
                    kb.wait(Sy, "dB")
                    kb.dma(Sy, dbg[nm], a32.ap("probe3"), "dD")
                    kb.wait(Sy, "dD")
                a32.free("probe3")
            if "qh0" in dbg:
                a32.alloc("probe2", 2048)
                for nm, h in (("qh0", 0), ("qh7", 7)):
                    kb.wait(Sy, "dA")
                    kb.dma(Sy, a32.ap("probe2"), qsp_d[h * QHD:h * QHD + 128, :], "dA")
                    kb.wait(Sy, "dA")
                    kb.dma(Sy, dbg[nm], a32.ap("probe2"), "dD")
                    kb.wait(Sy, "dD")
                    kb.dma(Sy, a32.ap("probe2", 0, 2048, 0, 64),
                           qsp_d[h * QHD + 128:h * QHD + 192, :], "dA")
                    kb.wait(Sy, "dA")
                    kb.dma(Sy, dbg[nm + "r"], a32.ap("probe2", 0, 2048, 0, 64), "dD")
                    kb.wait(Sy, "dD")
                a32.free("probe2")

        kb.finalize()
    return nc, kb



def _own_rows(core):
    h2 = core % 2
    return np.concatenate([np.arange(512 * j + 256 * h2, 512 * j + 256 * h2 + 256)
                           for j in range(8)])


def kernel(L0_attn_hidden_states, L0_attn_q_proj_w, L0_attn_kv_a_proj_w,
           L0_attn_kv_a_norm_w, L0_attn_kv_b_proj_w, L0_attn_o_proj_w,
           L0_ffn_w1, L0_ffn_g1, L0_ffn_w2, L0_ffn_g2, L0_ffn_w3, L0_ffn_g3):
    x = np.asarray(L0_attn_hidden_states, np.float32)
    wqT = np.ascontiguousarray(np.asarray(L0_attn_q_proj_w, np.float32).T)
    waT = np.ascontiguousarray(np.asarray(L0_attn_kv_a_proj_w, np.float32).T)
    wkvbT = np.ascontiguousarray(np.asarray(L0_attn_kv_b_proj_w, np.float32).T)
    woT = np.ascontiguousarray(np.asarray(L0_attn_o_proj_w, np.float32).T)
    w1T = np.ascontiguousarray(np.asarray(L0_ffn_w1, np.float32).T)
    w2T = np.ascontiguousarray(np.asarray(L0_ffn_w2, np.float32).T)
    w3T = np.ascontiguousarray(np.asarray(L0_ffn_w3, np.float32).T)
    kvnw = np.ascontiguousarray(np.asarray(L0_attn_kv_a_norm_w, np.float32).reshape(1, RANK))
    g1 = np.ascontiguousarray(np.asarray(L0_ffn_g1, np.float32).reshape(1, D))
    g2 = np.ascontiguousarray(np.asarray(L0_ffn_g2, np.float32).reshape(1, D))
    g3 = np.ascontiguousarray(np.asarray(L0_ffn_g3, np.float32).reshape(1, DH))
    maps = []
    for c in range(8):
        b = c // 2
        rows = _own_rows(c)
        maps.append({
            "xfull": np.ascontiguousarray(x[b]),
            "xown": np.ascontiguousarray(x[b][rows]),
            "qpos": np.ascontiguousarray(rows.reshape(16, 128).T.astype(np.float32)),
            "qposf": np.ascontiguousarray(
                np.broadcast_to(rows[None, :], (64, S // 2)).astype(np.float32)),
            "wqT": wqT, "waT": waT, "wkvbT": wkvbT, "woT": woT,
            "w1T": w1T, "w2T": w2T, "w3T": w3T,
            "kvnw": kvnw, "g1": g1, "g2": g2, "g3": g3,
        })
    nc, _kb = build_program()
    res = run_bass_kernel_spmd(nc, maps, core_ids=list(range(8)))
    out = np.zeros((B, S, D), np.float32)
    for c in range(8):
        out[c // 2][_own_rows(c)] = res.results[c]["out"]
    return out
